# version 17
# baseline (speedup 1.0000x reference)
import numpy as np
import ml_dtypes
import concourse.bass as bass
import concourse.mybir as mybir
from concourse import dve_ops
from concourse.bass_utils import run_bass_kernel_spmd
from concourse.bass_core_rust import IndirectOffsetOnAxis
from contextlib import ExitStack

F32 = mybir.dt.float32
BF16 = mybir.dt.bfloat16
I32 = mybir.dt.int32
AF = mybir.ActivationFunctionType
ALU = mybir.AluOpType

B, T, C, E, FF = 4, 4096, 1024, 8, 4096
TC = 512
NCH = T // TC
DH = 512
TH = 2048
PAD = 384
NEG = -1.0e38
EPS = 1e-5
BIS = 28
GRP8 = [[0, 1, 2, 3, 4, 5, 6, 7]]
PAIRS = [[0, 1], [2, 3], [4, 5], [6, 7]]
_STAGE = 8


class Ctr:
    def __init__(self, sem):
        self.sem = sem
        self.n = 0

    def inc(self, ins, k=1):
        ins.then_inc(self.sem, k)
        self.n += k
        return self.n


def build_nc():
    nc = bass.Bass("TRN2", target_bir_lowering=False)

    def din(name, shape, dtype=F32):
        return nc.dram_tensor(name, shape, dtype, kind="ExternalInput").ap()

    def dint(name, shape, dtype=F32):
        return nc.dram_tensor(name, shape, dtype, kind="Internal").ap()

    xT = din("xT", [C, T])
    xres = din("xres", [TH, C])
    wkT = din("wkT", [C, DH]); wvT = din("wvT", [C, DH]); wrT = din("wrT", [C, DH])
    woT = din("woT", [C, C])
    tdtf = din("tdtf", [128, 8])
    tmix = din("tmix", [128, 24])
    lngb = din("lngb", [128, 16])
    lnbb = din("lnbb", [128, 8], BF16)
    centT = din("centT", [C, E])
    w1 = din("w1", [C, FF]); b1cc = din("b1cc", [128, 32])
    w2 = din("w2", [FF, C]); b2row = din("b2row", [1, C])
    blockind = din("blockind", [128, 8])
    ltri16 = din("ltri16", [128, 128])
    cmask = din("cmask", [8, 1])
    stile = din("stile", [128, 16])
    onescol = din("onescol", [128, 1])
    onescolb = din("onescolb", [128, 1], BF16)
    onesrow1 = din("onesrow1", [1, 128])
    ones128 = din("ones128", [128, 128])
    identbf = din("identbf", [128, 128], BF16)
    ident16 = din("ident16", [16, 16])
    epscol = din("epscol", [128, 1])
    wkvidx = din("wkvidx", [128, 32], I32)
    rowidx = din("rowidx", [16, 1], I32)
    selmat = din("selmat", [128, 16])
    expoff = din("expoff", [1, 1])
    out = nc.dram_tensor("out", [TH, C], F32, kind="ExternalOutput").ap()

    x1d = dint("x1d", [TH, C])
    t1d = dint("t1d", [TH, C])
    wkv_in = dint("wkv_in", [NCH * DH, TC], BF16)
    wkv_all = dint("wkv_all", [NCH * 2 * DH, TC], BF16)
    scores_in = dint("scores_in", [E, TH])
    scores_g = dint("scores_g", [8 * E, TH])
    rk_bounce = dint("rk_bounce", [128, 256])
    crow_bounce = dint("crow_bounce", [128, 1])
    routed_loc = dint("routed_loc", [E * PAD + 64, C], BF16)
    routed_gall = dint("routed_gall", [E * 8 * PAD, C], BF16)
    h_loc = dint("h_loc", [TH, C], BF16)
    h_g = dint("h_g", [8 * TH, C], BF16)

    gctx = ExitStack()

    def sbg(name, shape, dtype=F32):
        return gctx.enter_context(nc.sbuf_tensor(name, shape, dtype))

    c_ones = sbg("c_ones", [128, 1])
    c_onesb = sbg("c_onesb", [128, 1], BF16)
    c_ones1r = sbg("c_ones1r", [1, 128])
    c_ones128 = sbg("c_ones128", [128, 128])
    c_identbf = sbg("c_identbf", [128, 128], BF16)
    c_ident16 = sbg("c_ident16", [16, 16])
    c_blockind = sbg("c_blockind", [128, 8])
    c_ltri = sbg("c_ltri", [128, 128])
    c_cmask = sbg("c_cmask", [8, 1])
    c_stile = sbg("c_stile", [128, 16])
    c_tdtf = sbg("c_tdtf", [128, 8])
    c_w = sbg("c_w", [128, 4])
    c_u = sbg("c_u", [128, 4])
    c_tmix = sbg("c_tmix", [128, 24])
    c_lngb = sbg("c_lngb", [128, 16])
    c_lnbb = sbg("c_lnbb", [128, 8], BF16)
    c_centg = sbg("c_centg", [128, 8 * E], BF16)
    c_sb8 = sbg("c_sb8", [8, 1])
    c_b2rep = sbg("c_b2rep", [128, C])
    c_b1c = sbg("c_b1c", [128, 32])
    c_eps = sbg("c_eps", [128, 1])
    c_wkvidx = sbg("c_wkvidx", [128, 32], I32)
    c_rowidx = sbg("c_rowidx", [16, 1], I32)
    c_selmat = sbg("c_selmat", [128, 16])
    c_expoff = sbg("c_expoff", [1, 1])

    psA = [gctx.enter_context(nc.psum_tensor(f"psA{i}", [128, TC], F32)) for i in range(4)]
    psB = [gctx.enter_context(nc.psum_tensor(f"psB{i}", [128, TC], F32)) for i in range(2)]
    psT0 = gctx.enter_context(nc.psum_tensor("psT0", [128, 128], BF16))
    psT1 = gctx.enter_context(nc.psum_tensor("psT1", [128, 128], BF16))

    LD = Ctr(gctx.enter_context(nc.semaphore("LD")))
    SD = Ctr(gctx.enter_context(nc.semaphore("SD")))
    GD = Ctr(gctx.enter_context(nc.semaphore("GD")))
    CC = Ctr(gctx.enter_context(nc.semaphore("CC")))
    V = Ctr(gctx.enter_context(nc.semaphore("V")))
    S = Ctr(gctx.enter_context(nc.semaphore("S")))
    P = Ctr(gctx.enter_context(nc.semaphore("P")))
    G = Ctr(gctx.enter_context(nc.semaphore("G")))
    G0 = Ctr(gctx.enter_context(nc.semaphore("G0")))
    GW = Ctr(gctx.enter_context(nc.semaphore("GW")))
    XL = Ctr(gctx.enter_context(nc.semaphore("XL")))
    WKVL = Ctr(gctx.enter_context(nc.semaphore("WKVL")))
    BNC = Ctr(gctx.enter_context(nc.semaphore("BNC")))
    CRW = Ctr(gctx.enter_context(nc.semaphore("CRW")))
    SCT = Ctr(gctx.enter_context(nc.semaphore("SCT")))
    GTH = Ctr(gctx.enter_context(nc.semaphore("GTH")))
    MIX = Ctr(gctx.enter_context(nc.semaphore("MIX")))
    W1L = Ctr(gctx.enter_context(nc.semaphore("W1L")))
    W1M = Ctr(gctx.enter_context(nc.semaphore("W1M")))
    W1N = Ctr(gctx.enter_context(nc.semaphore("W1N")))
    W1P = Ctr(gctx.enter_context(nc.semaphore("W1P")))
    W2L = Ctr(gctx.enter_context(nc.semaphore("W2L")))
    HM = Ctr(gctx.enter_context(nc.semaphore("HM")))
    XR = Ctr(gctx.enter_context(nc.semaphore("XR")))
    X1L = Ctr(gctx.enter_context(nc.semaphore("X1L")))
    TTL = Ctr(gctx.enter_context(nc.semaphore("TTL")))
    ZL = Ctr(gctx.enter_context(nc.semaphore("ZL")))
    FXL = Ctr(gctx.enter_context(nc.semaphore("FXL")))
    OUTS = Ctr(gctx.enter_context(nc.semaphore("OUTS")))
    WKS = Ctr(gctx.enter_context(nc.semaphore("WKS")))
    X1S = Ctr(gctx.enter_context(nc.semaphore("X1S")))
    T1S = Ctr(gctx.enter_context(nc.semaphore("T1S")))
    T1L = Ctr(gctx.enter_context(nc.semaphore("T1L")))
    T2S = Ctr(gctx.enter_context(nc.semaphore("T2S")))
    HS = Ctr(gctx.enter_context(nc.semaphore("HS")))

    blk = gctx.enter_context(nc.Block())

    @blk.sync
    def _(sync):
        for dst, src in [(c_ones, onescol), (c_onesb, onescolb),
                         (c_ones1r, onesrow1), (c_ones128, ones128),
                         (c_identbf, identbf),
                         (c_ident16, ident16), (c_blockind, blockind),
                         (c_ltri, ltri16), (c_cmask, cmask), (c_stile, stile),
                         (c_tdtf, tdtf), (c_tmix, tmix), (c_lngb, lngb),
                         (c_lnbb, lnbb), (c_b1c, b1cc), (c_eps, epscol), (c_wkvidx, wkvidx), (c_rowidx, rowidx),
                         (c_selmat, selmat), (c_expoff, expoff)]:
            LD.inc(sync.dma_start(out=dst[:], in_=src), 16)

    @blk.gpsimd
    def _(gpsimd):
        G0.inc(gpsimd.dma_start(
            out=c_centg[:].rearrange("p (a f) -> p a f", a=8),
            in_=centT.rearrange("(a p) f -> p a f", p=128)), 16)
        G0.inc(gpsimd.dma_start(out=c_b2rep[0:1, :], in_=b2row), 16)
    ld0, gd0 = LD.n, G0.n

    @blk.tensor
    def _(tensor):
        tensor.wait_ge(LD.sem, ld0)
        tensor.wait_ge(G0.sem, gd0)
        for ci in range(8):
            _mm = tensor.matmul(psB[0][0:8, 0:1], c_centg[:, ci * E:(ci + 1) * E],
                          c_lnbb[:, ci:ci + 1], start=(ci == 0), stop=(ci == 7))
        for j in range(2):
            _mm = tensor.matmul(psA[j][:, :], c_ones1r[:], c_b2rep[0:1, j * TC:(j + 1) * TC],
                          start=True, stop=True)
        P.inc(_mm)
    p_const = P.n

    @blk.vector
    def _(vector):
        vector.wait_ge(P.sem, p_const)
        V.inc(vector.tensor_copy(c_sb8[:], psB[0][0:8, 0:1]))
        V.inc(vector.tensor_copy(c_b2rep[:, 0:TC], psA[0][:, :]))
        w = V.inc(vector.tensor_copy(c_b2rep[:, TC:C], psA[1][:, :]))
        vector.wait_ge(V.sem, w)
        for ci in range(8):
            V.inc(vector.tensor_scalar_mul(c_centg[:, ci * E:(ci + 1) * E],
                                           c_centg[:, ci * E:(ci + 1) * E],
                                           c_lngb[:, ci:ci + 1]))
        V.inc(vector.nop())
    v_const = V.n

    @blk.scalar
    def _(scalar):
        scalar.wait_ge(LD.sem, ld0)
        w = S.inc(scalar.activation(c_w[:], c_tdtf[:, 0:4], AF.Exp))
        scalar.wait_ge(S.sem, w)
        w = S.inc(scalar.mul(c_w[:], c_w[:], -1.0))
        S.inc(scalar.copy(c_u[:], c_tdtf[:, 4:8]))
    s_const = S.n

    actx = ExitStack()

    def sba(name, shape, dtype=F32):
        return actx.enter_context(nc.sbuf_tensor(name, shape, dtype))

    c_wkTs = sba("c_wkTs", [128, 8 * DH], BF16)
    c_wvTs = sba("c_wvTs", [128, 8 * DH], BF16)
    c_wrTs = sba("c_wrTs", [128, 8 * DH], BF16)
    c_woTs = sba("c_woTs", [128, 8 * C], BF16)
    a_xb = sba("a_xb", [128, 8 * TC], BF16)
    a_xh = sba("a_xh", [128, 8 * (TC + 1)], BF16)
    a_hc = sba("a_hc", [128, 8], BF16)
    a_dall = sba("a_dall", [128, 8 * TC], BF16)
    a_mx = sba("a_mx", [128, 2 * 8 * TC], BF16)
    a_rows = sba("a_rows", [1, 4 * TC])
    a_repm = sba("a_repm", [128, TC], BF16)
    a_repr = sba("a_repr", [128, TC], BF16)
    a_k = sba("a_k", [128, 4 * TC])
    a_v = sba("a_v", [128, 4 * TC])
    a_sr = sba("a_sr", [128, 4 * TC], BF16)
    a_p = sba("a_p", [128, 4 * (TC + 1)])
    a_aa = sba("a_aa", [128, 4 * (TC + 1)])
    a_bb = sba("a_bb", [128, 4 * (TC + 1)])
    a_pst = sba("a_pst", [128, 12])
    a_s1 = sba("a_s1", [128, 4 * TC]); a_s2 = sba("a_s2", [128, 4 * TC])
    a_s3 = sba("a_s3", [128, 4 * TC]); a_s4 = sba("a_s4", [128, 4 * TC])
    a_s5 = sba("a_s5", [128, 4 * TC])
    a_srw = sba("a_srw", [128, 4 * TC], BF16)
    a_wkvf = sba("a_wkvf", [128, 8 * TC], BF16)
    a_wout = sba("a_wout", [128, C])
    a_xres = sba("a_xres", [128, C])
    a_x1 = sba("a_x1", [128, C])
    a_st = sba("a_st", [128, 8])

    @blk.gpsimd
    def _(gpsimd):
        for dst, src in [(c_wkTs, wkT), (c_wvTs, wvT), (c_wrTs, wrT)]:
            GW.inc(gpsimd.dma_start(
                out=dst[:].rearrange("p (a f) -> p a f", a=8),
                in_=src.rearrange("(a p) f -> p a f", p=128)), 16)
        GW.inc(gpsimd.dma_start(
            out=c_woTs[:].rearrange("p (a f) -> p a f", a=8),
            in_=woT.rearrange("(a p) f -> p a f", p=128)), 16)
    gd_w = GW.n

    @blk.vector
    def _(vector):
        V.inc(vector.memset(a_pst[:, 0:4], NEG))
        V.inc(vector.memset(a_pst[:, 4:12], 0.0))
        V.inc(vector.memset(a_hc[:], 0.0))

    sd_wkv = []
    cc_wkv = []
    p_mix_done = [P.n]

    def emit_ccwkv(t):
        @blk.gpsimd
        def _(gpsimd, t=t):
            gpsimd.wait_ge(WKS.sem, sd_wkv[t])
            CC.inc(gpsimd.collective_compute(
                "AllGather", ALU.bypass, replica_groups=PAIRS,
                ins=[wkv_in[t * DH:(t + 1) * DH, :].opt()],
                outs=[wkv_all[t * 2 * DH:(t + 1) * 2 * DH, :].opt()]))
        cc_wkv.append(CC.n)

    v_xh_list = []
    for tch in range(NCH):
        @blk.gpsimd
        def _(gpsimd, tch=tch):
            if tch == 0:
                gpsimd.wait_ge(V.sem, v_const)
                gpsimd.wait_ge(S.sem, s_const)
            else:
                gpsimd.wait_ge(P.sem, p_mix_done[-1])
            XL.inc(gpsimd.dma_start(
                out=a_xb[:].rearrange("p (a f) -> p a f", a=8),
                in_=xT.rearrange("(a p) t -> p a t", p=128)[:, :, tch * TC:(tch + 1) * TC]), 16)
        ld_x = XL.n

        @blk.tensor
        def _(tensor, ld_x=ld_x, tch=tch):
            tensor.wait_ge(XL.sem, ld_x)
            if tch == 0:
                tensor.wait_ge(GW.sem, gd_w)
                tensor.wait_ge(V.sem, v_const)
            for ci in range(8):
                _mm = tensor.matmul(psB[0][0:1, :], c_onesb[:],
                              a_xb[:, ci * TC:(ci + 1) * TC],
                              start=(ci == 0), stop=(ci == 7))
            P.inc(_mm)

        @blk.scalar
        def _(scalar, ld_x=ld_x):
            scalar.wait_ge(XL.sem, ld_x)
            for ci in range(8):
                S.inc(scalar.activation(a_xh[:, ci * (TC + 1) + 1:ci * (TC + 1) + 1 + TC],
                                        a_xb[:, ci * TC:(ci + 1) * TC], AF.Square))
        s_sq = S.n

        @blk.tensor
        def _(tensor, s_sq=s_sq):
            tensor.wait_ge(S.sem, s_sq)
            for ci in range(8):
                _mm = tensor.matmul(psB[1][0:1, :], c_onesb[:],
                              a_xh[:, ci * (TC + 1) + 1:ci * (TC + 1) + 1 + TC],
                              start=(ci == 0), stop=(ci == 7))
            P.inc(_mm)
        p_sq = P.n

        @blk.vector
        def _(vector, p_sq=p_sq):
            vector.wait_ge(P.sem, p_sq)
            mu = a_rows[0:1, 0:TC]; sq = a_rows[0:1, TC:2 * TC]
            var = a_rows[0:1, 2 * TC:3 * TC]
            V.inc(vector.tensor_scalar_mul(mu, psB[0][0:1, :], 1.0 / C))
            w = V.inc(vector.tensor_scalar_mul(sq, psB[1][0:1, :], 1.0 / C))
            vector.wait_ge(V.sem, w)
            w = V.inc(vector.tensor_mul(var, mu, mu))
            vector.wait_ge(V.sem, w)
            V.inc(vector.tensor_sub(var, sq, var))
        v_var = V.n

        @blk.scalar
        def _(scalar, v_var=v_var):
            scalar.wait_ge(V.sem, v_var)
            var = a_rows[0:1, 2 * TC:3 * TC]; rst = a_rows[0:1, TC:2 * TC]
            lnv = a_rows[0:1, 3 * TC:4 * TC]
            w = S.inc(scalar.activation(lnv, var, AF.Ln, bias=c_eps[0:1, 0:1]))
            scalar.wait_ge(S.sem, w)
            S.inc(scalar.activation(rst, lnv, AF.Exp, scale=-0.5))
        s_rst = S.n

        @blk.vector
        def _(vector, s_rst=s_rst):
            vector.wait_ge(S.sem, s_rst)
            mu = a_rows[0:1, 0:TC]; rst = a_rows[0:1, TC:2 * TC]
            nmr = a_rows[0:1, 3 * TC:4 * TC]
            w = V.inc(vector.tensor_mul(nmr, mu, rst))
            vector.wait_ge(V.sem, w)
            V.inc(vector.tensor_scalar_mul(nmr, nmr, -1.0))
        v_nmr = V.n

        @blk.tensor
        def _(tensor, v_nmr=v_nmr):
            tensor.wait_ge(V.sem, v_nmr)
            _mm = tensor.matmul(psB[0][:, :], c_ones1r[:], a_rows[0:1, 3 * TC:4 * TC],
                          start=True, stop=True)
            _mm = tensor.matmul(psB[1][:, :], c_ones1r[:], a_rows[0:1, TC:2 * TC],
                          start=True, stop=True)
            P.inc(_mm)
        p_bc = P.n

        @blk.vector
        def _(vector, p_bc=p_bc):
            vector.wait_ge(P.sem, p_bc)
            V.inc(vector.tensor_copy(a_repm[:], psB[0][:, :]))
            w = V.inc(vector.tensor_copy(a_repr[:], psB[1][:, :]))
            vector.wait_ge(V.sem, w)
            for ci in range(8):
                xh = a_xh[:, ci * (TC + 1) + 1:ci * (TC + 1) + 1 + TC]
                w = V.inc(vector.tensor_add(xh, a_xb[:, ci * TC:(ci + 1) * TC], a_repm[:]))
                vector.wait_ge(V.sem, w)
                V.inc(vector.tensor_mul(xh, xh, a_repr[:]))
            for ci in range(8):
                V.inc(vector.tensor_copy(a_xh[:, ci * (TC + 1):ci * (TC + 1) + 1],
                                         a_hc[:, ci:ci + 1]))
            w = V.inc(vector.nop())
            vector.wait_ge(V.sem, w)
            for ci in range(8):
                V.inc(vector.tensor_sub(a_dall[:, ci * TC:(ci + 1) * TC],
                                        a_xh[:, ci * (TC + 1) + 1:ci * (TC + 1) + 1 + TC],
                                        a_xh[:, ci * (TC + 1):ci * (TC + 1) + TC]))
            w = V.inc(vector.nop())
            vector.wait_ge(V.sem, w)
            for ci in range(8):
                V.inc(vector.tensor_copy(a_hc[:, ci:ci + 1],
                                         a_xh[:, ci * (TC + 1) + TC:ci * (TC + 1) + TC + 1]))
        v_xh = V.n
        v_xh_list.append(v_xh)

        for pi, (tmo, wsb) in enumerate([(0, c_wkTs), (8, c_wvTs), (16, c_wrTs)]):
            mxo = ((tch * 3 + pi) % 2) * (8 * TC)

            @blk.vector
            def _(vector, tmo=tmo, mxo=mxo, v_xh=v_xh):
                vector.wait_ge(V.sem, v_xh)
                vector.wait_ge(P.sem, p_mix_done[max(0, len(p_mix_done) - 2)])
                for ci in range(8):
                    V.inc(vector.scalar_tensor_tensor(
                        a_mx[:, mxo + ci * TC:mxo + (ci + 1) * TC],
                        a_dall[:, ci * TC:(ci + 1) * TC],
                        c_tmix[:, tmo + ci:tmo + ci + 1],
                        a_xh[:, ci * (TC + 1):ci * (TC + 1) + TC],
                        ALU.mult, ALU.add))
            v_mx = V.n

            @blk.tensor
            def _(tensor, mxo=mxo, v_mx=v_mx, wsb=wsb):
                tensor.wait_ge(V.sem, v_mx)
                for dt in range(4):
                    for ci in range(8):
                        _mm = tensor.matmul(psA[dt][:, :],
                                      wsb[:, ci * DH + dt * 128:ci * DH + (dt + 1) * 128],
                                      a_mx[:, mxo + ci * TC:mxo + (ci + 1) * TC],
                                      start=(ci == 0), stop=(ci == 7))
                P.inc(_mm)
            p_mix_done.append(P.n)

            if pi < 2:
                dst = a_k if pi == 0 else a_v

                @blk.vector
                def _(vector, dst=dst, pp=P.n):
                    vector.wait_ge(P.sem, pp)
                    for dt in range(4):
                        V.inc(vector.tensor_copy(dst[:, dt * TC:(dt + 1) * TC],
                                                 psA[dt][:, :]))
            else:
                @blk.scalar
                def _(scalar, pp=P.n):
                    scalar.wait_ge(P.sem, pp)
                    for dt in range(4):
                        S.inc(scalar.activation(a_sr[:, dt * TC:(dt + 1) * TC],
                                                psA[dt][:, :], AF.Sigmoid))
        v_kv = V.n
        s_sr = S.n

        if tch >= 1:
            emit_ccwkv(tch - 1)

        def sl(dt):
            return slice(dt * TC, (dt + 1) * TC)

        def pf(dt, a=0, b=0):
            return slice(dt * (TC + 1) + a, dt * (TC + 1) + b)

        @blk.vector
        def _(vector, v_kv=v_kv):
            vector.wait_ge(V.sem, v_kv)
            for dt in range(4):
                V.inc(vector.tensor_copy(a_p[:, pf(dt, 0, 1)], a_pst[:, dt:dt + 1]))
            vector.wait_ge(V.sem, V.n)
            for dt in range(4):
                V.inc(vector.tensor_tensor_scan(
                    a_p[:, pf(dt, 1, TC + 1)],
                    c_w[:, dt:dt + 1].to_broadcast([128, TC]), a_k[:, sl(dt)],
                    a_pst[:, dt:dt + 1], ALU.add, ALU.max))
            vector.wait_ge(V.sem, V.n)
            for dt in range(4):
                V.inc(vector.tensor_copy(a_pst[:, dt:dt + 1], a_p[:, pf(dt, TC, TC + 1)]))
                V.inc(vector.scalar_tensor_tensor(
                    a_s1[:, sl(dt)], a_p[:, pf(dt, 0, TC)], c_w[:, dt:dt + 1],
                    a_p[:, pf(dt, 1, TC + 1)], ALU.add, ALU.subtract))
                V.inc(vector.tensor_sub(a_s2[:, sl(dt)], a_k[:, sl(dt)],
                                        a_p[:, pf(dt, 1, TC + 1)]))
                V.inc(vector.scalar_tensor_tensor(
                    a_s5[:, sl(dt)], a_p[:, pf(dt, 0, TC)], c_u[:, dt:dt + 1],
                    a_k[:, sl(dt)], ALU.subtract, ALU.subtract))
        v_pf = V.n

        @blk.scalar
        def _(scalar, v_pf=v_pf):
            scalar.wait_ge(V.sem, v_pf)
            S.inc(scalar.activation(a_s1[:, :], a_s1[:, :], AF.Exp))
            S.inc(scalar.activation(a_s2[:, :], a_s2[:, :], AF.Exp))
            S.inc(scalar.activation(a_s5[:, :], a_s5[:, :], AF.Exp))
        s_exp = S.n

        @blk.vector
        def _(vector, s_exp=s_exp):
            vector.wait_ge(S.sem, s_exp)
            for dt in range(4):
                V.inc(vector.tensor_mul(a_s3[:, sl(dt)], a_s2[:, sl(dt)], a_v[:, sl(dt)]))
                V.inc(vector.tensor_copy(a_aa[:, pf(dt, 0, 1)], a_pst[:, 4 + dt:5 + dt]))
                V.inc(vector.tensor_copy(a_bb[:, pf(dt, 0, 1)], a_pst[:, 8 + dt:9 + dt]))
            vector.wait_ge(V.sem, V.n)
            for dt in range(4):
                V.inc(vector.tensor_tensor_scan(
                    a_aa[:, pf(dt, 1, TC + 1)], a_s1[:, sl(dt)], a_s3[:, sl(dt)],
                    a_pst[:, 4 + dt:5 + dt], ALU.mult, ALU.add))
                V.inc(vector.tensor_tensor_scan(
                    a_bb[:, pf(dt, 1, TC + 1)], a_s1[:, sl(dt)], a_s2[:, sl(dt)],
                    a_pst[:, 8 + dt:9 + dt], ALU.mult, ALU.add))
            vector.wait_ge(V.sem, V.n)
            for dt in range(4):
                V.inc(vector.tensor_copy(a_pst[:, 4 + dt:5 + dt], a_aa[:, pf(dt, TC, TC + 1)]))
                V.inc(vector.tensor_copy(a_pst[:, 8 + dt:9 + dt], a_bb[:, pf(dt, TC, TC + 1)]))
                V.inc(vector.tensor_mul(a_s3[:, sl(dt)], a_s5[:, sl(dt)], a_aa[:, pf(dt, 0, TC)]))
            vector.wait_ge(V.sem, V.n)
            for dt in range(4):
                V.inc(vector.tensor_add(a_s3[:, sl(dt)], a_s3[:, sl(dt)], a_v[:, sl(dt)]))
                V.inc(vector.tensor_mul(a_s4[:, sl(dt)], a_s5[:, sl(dt)], a_bb[:, pf(dt, 0, TC)]))
        v_nd = V.n

        @blk.scalar
        def _(scalar, v_nd=v_nd):
            scalar.wait_ge(V.sem, v_nd)
            w = S.inc(scalar.activation(a_s1[:, :], a_s4[:, :], AF.Ln, bias=1.0))
            scalar.wait_ge(S.sem, w)
            S.inc(scalar.activation(a_s4[:, :], a_s1[:, :], AF.Exp, scale=-1.0))
        s_rd = S.n

        @blk.vector
        def _(vector, s_rd=s_rd, s_sr=s_sr):
            vector.wait_ge(S.sem, s_rd)
            vector.wait_ge(S.sem, s_sr)
            if sd_wkv:
                vector.wait_ge(WKS.sem, sd_wkv[-1])
            w = V.inc(vector.tensor_mul(a_s3[:, :], a_s3[:, :], a_s4[:, :]))
            vector.wait_ge(V.sem, w)
            V.inc(vector.tensor_mul(a_srw[:, :], a_s3[:, :], a_sr[:, :]))
        v_y = V.n

        @blk.scalar
        def _(scalar, tch=tch, v_y=v_y):
            scalar.wait_ge(V.sem, v_y)
            WKS.inc(scalar.dma_start(
                out=wkv_in[tch * DH:(tch + 1) * DH, :].rearrange(
                    "(dt p) t -> p dt t", p=128),
                in_=a_srw[:].rearrange("p (dt t) -> p dt t", dt=4)), 16)
        sd_wkv.append(WKS.n)

    emit_ccwkv(NCH - 1)

    sd_x1 = []
    v_x1_list = [0]
    v_ws_list = []
    p_wo_list = [0]
    for q in range(4):
        @blk.gpsimd
        def _(gpsimd, q=q):
            gpsimd.wait_ge(CC.sem, cc_wkv[-1])
            gpsimd.wait_ge(P.sem, p_wo_list[-1])
            for dt in range(8):
                WKVL.inc(gpsimd.indirect_dma_start(
                    a_wkvf[:, dt * TC:(dt + 1) * TC], None,
                    wkv_all, IndirectOffsetOnAxis(c_wkvidx[:, q * 8 + dt:q * 8 + dt + 1], 0)), 16)
        gd_wkv = WKVL.n

        for sub in range(4):
            j = q * 4 + sub

            @blk.sync
            def _(sync, j=j):
                sync.wait_ge(V.sem, v_x1_list[-1])
                XR.inc(sync.dma_start(out=a_xres[:], in_=xres[j * 128:(j + 1) * 128, :]), 16)
            ld_xr = XR.n

            @blk.tensor
            def _(tensor, sub=sub, gd_wkv=gd_wkv, j=j):
                tensor.wait_ge(WKVL.sem, gd_wkv)
                if v_ws_list:
                    tensor.wait_ge(V.sem, v_ws_list[-1])
                for do in range(2):
                    for dt in range(8):
                        _mm = tensor.matmul(psB[do][:, :],
                                      a_wkvf[:, dt * TC + sub * 128:dt * TC + (sub + 1) * 128],
                                      c_woTs[:, dt * C + do * TC:dt * C + (do + 1) * TC],
                                      start=(dt == 0), stop=(dt == 7))
                P.inc(_mm)
            p_wo = P.n
            p_wo_list.append(p_wo)

            @blk.vector
            def _(vector, p_wo=p_wo):
                vector.wait_ge(P.sem, p_wo)
                V.inc(vector.tensor_copy(a_wout[:, 0:TC], psB[0][:, :]))
                w = V.inc(vector.tensor_copy(a_wout[:, TC:C], psB[1][:, :]))
                vector.wait_ge(V.sem, w)
                V.inc(vector.tensor_scalar(a_s1[:, 0:TC], a_wout[:, 0:TC], 0.0, 0.0,
                                           ALU.add, ALU.add, accum_out=a_st[:, 0:1]))
                V.inc(vector.tensor_scalar(a_s2[:, 0:TC], a_wout[:, TC:C], 0.0, 0.0,
                                           ALU.add, ALU.add, accum_out=a_st[:, 1:2]))
            v_ws = V.n
            v_ws_list.append(v_ws)

            @blk.scalar
            def _(scalar, v_ws=v_ws):
                scalar.wait_ge(V.sem, v_ws)
                S.inc(scalar.activation(a_s3[:, 0:TC], a_wout[:, 0:TC], AF.Square,
                                        accum_out=a_st[:, 2:3]))
                S.inc(scalar.activation(a_s4[:, 0:TC], a_wout[:, TC:C], AF.Square,
                                        accum_out=a_st[:, 3:4]))
            s_wsq = S.n

            @blk.vector
            def _(vector, s_wsq=s_wsq):
                vector.wait_ge(S.sem, s_wsq)
                w = V.inc(vector.tensor_add(a_st[:, 4:5], a_st[:, 0:1], a_st[:, 1:2]))
                V.inc(vector.tensor_add(a_st[:, 5:6], a_st[:, 2:3], a_st[:, 3:4]))
                vector.wait_ge(V.sem, w + 1)
                w = V.inc(vector.tensor_scalar_mul(a_st[:, 4:5], a_st[:, 4:5], 1.0 / C))
                V.inc(vector.tensor_scalar_mul(a_st[:, 5:6], a_st[:, 5:6], 1.0 / C))
                vector.wait_ge(V.sem, w + 1)
                w = V.inc(vector.tensor_mul(a_st[:, 6:7], a_st[:, 4:5], a_st[:, 4:5]))
                vector.wait_ge(V.sem, w)
                V.inc(vector.tensor_sub(a_st[:, 5:6], a_st[:, 5:6], a_st[:, 6:7]))
            v_wst = V.n

            @blk.scalar
            def _(scalar, v_wst=v_wst):
                scalar.wait_ge(V.sem, v_wst)
                w = S.inc(scalar.activation(a_st[:, 6:7], a_st[:, 5:6], AF.Ln, bias=c_eps[:, 0:1]))
                scalar.wait_ge(S.sem, w)
                S.inc(scalar.activation(a_st[:, 7:8], a_st[:, 6:7], AF.Exp, scale=-0.5))
            s_wrst = S.n

            @blk.vector
            def _(vector, s_wrst=s_wrst, ld_xr=ld_xr):
                vector.wait_ge(S.sem, s_wrst)
                vector.wait_ge(XR.sem, ld_xr)
                if sd_x1:
                    vector.wait_ge(X1S.sem, sd_x1[-1])
                w = V.inc(vector.tensor_mul(a_st[:, 6:7], a_st[:, 4:5], a_st[:, 7:8]))
                vector.wait_ge(V.sem, w)
                w = V.inc(vector.tensor_scalar_mul(a_st[:, 6:7], a_st[:, 6:7], -1.0))
                vector.wait_ge(V.sem, w)
                w = V.inc(vector.tensor_scalar(a_x1[:], a_wout[:], a_st[:, 7:8],
                                               a_st[:, 6:7], ALU.mult, ALU.add))
                vector.wait_ge(V.sem, w)
                V.inc(vector.tensor_add(a_x1[:], a_x1[:], a_xres[:]))
            v_x1 = V.n
            v_x1_list.append(v_x1)

            @blk.scalar
            def _(scalar, j=j, v_x1=v_x1):
                scalar.wait_ge(V.sem, v_x1)
                X1S.inc(scalar.dma_start(out=x1d[j * 128:(j + 1) * 128, :], in_=a_x1[:]), 16)
            sd_x1.append(X1S.n)

    marks_A = dict(V=V.n, S=S.n, P=P.n, GD=GD.n, LD=LD.n, SD=SD.n, CC=CC.n,
                   XL=XL.n, WKVL=WKVL.n, XR=XR.n, X1S=X1S.n, WKS=WKS.n, GW=GW.n)
    actx.close()

    if _STAGE <= 3:
        @blk.gpsimd
        def _(gpsimd):
            gpsimd.wait_ge(X1S.sem, sd_x1[-1])
            for j in range(16):
                OUTS.inc(gpsimd.dma_start(out=out[j * 128:(j + 1) * 128, :],
                                          in_=x1d[j * 128:(j + 1) * 128, :]), 16)
            gpsimd.wait_ge(OUTS.sem, OUTS.n)
        gctx.close()
        return nc

    bctx = ExitStack()

    def sbb(name, shape, dtype=F32, **kw):
        return bctx.enter_context(nc.sbuf_tensor(name, shape, dtype, **kw))

    f_x1g = sbb("f_x1g", [128, 4 * C])
    f_t1bg = sbb("f_t1bg", [128, 2 * C])
    f_st = sbb("f_st", [128, 16])
    f_db = sbb("f_db", [128, 2 * C], BF16)
    f_t1T = sbb("f_t1T", [128, TH], BF16, is_transpose_dest=True)
    f_Z = sbb("f_Z", [128, 128])
    f_dmp = sbb("f_dmp", [128, 128], BF16)
    f_claim = sbb("f_claim", [128, 128])
    f_self = sbb("f_self", [128, 128])
    f_rowc = sbb("f_rowc", [128, 1])
    f_loC = sbb("f_loC", [128, 1])
    f_step = sbb("f_step", [128, 1])
    f_lo = sbb("f_lo", [1, BIS + 2])
    f_cc = sbb("f_cc", [1, 1])
    f_rank = sbb("f_rank", [128, 128])
    f_rank2 = sbb("f_rank2", [128, 128])
    f_slotm = sbb("f_slotm", [128, 256])
    f_mrows = sbb("f_mrows", [16, 256])
    f_sloti = sbb("f_sloti", [128, 16], I32)
    f_rall = sbb("f_rall", [128, 16])
    f_ralli = sbb("f_ralli", [128, 16], I32)
    f_cnt8 = sbb("f_cnt8", [8, 8])
    f_crow = sbb("f_crow", [1, 24])
    f_crowcol = sbb("f_crowcol", [128, 1])
    f_cm16 = sbb("f_cm16", [16, 1])
    f_cmine = sbb("f_cmine", [1, 16])
    f_cumrep = sbb("f_cumrep", [128, 17])
    f_idx2f = sbb("f_idx2f", [128, 16])
    f_idx2t = sbb("f_idx2t", [128, 16])
    f_idx2i = sbb("f_idx2i", [128, 16], I32)
    f_grows = sbb("f_grows", [128, C], BF16)
    f_normT = sbb("f_normT", [128, 8 * TH], BF16)
    f_h = sbb("f_h", [128, 32 * TC], BF16)
    f_w1t = sbb("f_w1t", [128, 4 * 8 * 128], BF16)
    f_b1e = sbb("f_b1e", [128, 32])
    f_hrow = sbb("f_hrow", [128, C], BF16)
    f_hmg = sbb("f_hmg", [128, 2 * C], BF16)
    f_t1all = sbb("f_t1all", [128, 16 * C], BF16)
    f_sc = sbb("f_sc", [8, TH], F32)
    f_w2r = sbb("f_w2r", [128, 32 * TC], BF16)

    def bwait(eng):
        eng.wait_ge(V.sem, marks_A["V"])
        eng.wait_ge(S.sem, marks_A["S"])
        eng.wait_ge(P.sem, marks_A["P"])
        eng.wait_ge(LD.sem, marks_A["LD"])
        eng.wait_ge(CC.sem, marks_A["CC"])
        eng.wait_ge(XL.sem, marks_A["XL"])
        eng.wait_ge(WKVL.sem, marks_A["WKVL"])
        eng.wait_ge(XR.sem, marks_A["XR"])
        eng.wait_ge(X1S.sem, marks_A["X1S"])
        eng.wait_ge(WKS.sem, marks_A["WKS"])

    def st(k, slot):
        return f_st[:, k * 8 + slot:k * 8 + slot + 1]

    def vstats(vector, ks):
        w0 = V.n
        for k in ks:
            V.inc(vector.tensor_scalar_mul(st(k, 2), st(k, 0), 1.0 / C))
            V.inc(vector.tensor_scalar_mul(st(k, 3), st(k, 1), 1.0 / C))
        vector.wait_ge(V.sem, V.n)
        for k in ks:
            V.inc(vector.tensor_mul(st(k, 4), st(k, 2), st(k, 2)))
        vector.wait_ge(V.sem, V.n)
        for k in ks:
            V.inc(vector.tensor_sub(st(k, 3), st(k, 3), st(k, 4)))

    def slnexp(scalar, ks):
        for k in ks:
            S.inc(scalar.activation(st(k, 4), st(k, 3), AF.Ln, bias=c_eps[:, 0:1]))
        scalar.wait_ge(S.sem, S.n)
        for k in ks:
            S.inc(scalar.activation(st(k, 5), st(k, 4), AF.Exp, scale=-0.5))

    def vnorm(vector, ks, src_of_k, dst_of_k):
        for k in ks:
            V.inc(vector.tensor_mul(st(k, 6), st(k, 2), st(k, 5)))
        vector.wait_ge(V.sem, V.n)
        for k in ks:
            V.inc(vector.tensor_scalar_mul(st(k, 6), st(k, 6), -1.0))
        vector.wait_ge(V.sem, V.n)
        for k in ks:
            V.inc(vector.tensor_scalar(dst_of_k(k), src_of_k(k),
                                       st(k, 5), st(k, 6), ALU.mult, ALU.add))

    sd_t1 = []
    v_t1_list = [0]
    v_su_list = []
    for g in range(8):
        par = g % 2
        j0 = 2 * g

        def x1t(k, par=par):
            return f_x1g[:, (par * 2 + k) * C:(par * 2 + k + 1) * C]

        def t1t(k):
            return f_t1bg[:, k * C:(k + 1) * C]

        @blk.sync
        def _(sync, g=g, j0=j0, par=par):
            if g == 0:
                bwait(sync)
            if g >= 1:
                sync.wait_ge(V.sem, v_su_list[-1])
            sync.wait_ge(X1S.sem, sd_x1[j0 + 1])
            for k in range(2):
                X1L.inc(sync.dma_start(out=x1t(k),
                                       in_=x1d[(j0 + k) * 128:(j0 + k + 1) * 128, :]), 16)
        ld_j = X1L.n

        @blk.vector
        def _(vector, g=g, ld_j=ld_j, x1t=x1t):
            if g == 0:
                bwait(vector)
            vector.wait_ge(X1L.sem, ld_j)
            for k in range(2):
                V.inc(vector.tensor_scalar(f_db[:, k * C:(k + 1) * C], x1t(k), 0.0, 0.0, ALU.add, ALU.add,
                                           accum_out=st(k, 0)))
        v_su = V.n
        v_su_list.append(v_su)

        @blk.scalar
        def _(scalar, g=g, ld_j=ld_j, v_su=v_su, x1t=x1t):
            if g == 0:
                bwait(scalar)
            scalar.wait_ge(X1L.sem, ld_j)
            scalar.wait_ge(V.sem, v_su)
            for k in range(2):
                S.inc(scalar.activation(f_db[:, k * C:(k + 1) * C], x1t(k), AF.Square, accum_out=st(k, 1)))
        s_su = S.n

        @blk.vector
        def _(vector, s_su=s_su):
            vector.wait_ge(S.sem, s_su)
            vstats(vector, range(2))
        v_st2 = V.n

        @blk.scalar
        def _(scalar, v_st2=v_st2):
            scalar.wait_ge(V.sem, v_st2)
            slnexp(scalar, range(2))
        s_st2 = S.n

        @blk.vector
        def _(vector, s_st2=s_st2, x1t=x1t, t1t=t1t):
            vector.wait_ge(S.sem, s_st2)
            if sd_t1:
                vector.wait_ge(T1S.sem, sd_t1[-1])
            vnorm(vector, range(2), x1t, t1t)
            vector.wait_ge(V.sem, V.n)
            for k in range(2):
                V.inc(vector.tensor_scalar(f_db[:, k * C:(k + 1) * C], t1t(k), 0.0, 0.0, ALU.add, ALU.add,
                                           accum_out=st(k, 0)))
        v_t1a = V.n

        @blk.scalar
        def _(scalar, j0=j0, v_t1a=v_t1a, t1t=t1t):
            scalar.wait_ge(V.sem, v_t1a)
            for k in range(2):
                T1S.inc(scalar.dma_start(out=t1d[(j0 + k) * 128:(j0 + k + 1) * 128, :],
                                         in_=t1t(k)), 16)
            for k in range(2):
                S.inc(scalar.activation(f_db[:, k * C:(k + 1) * C], t1t(k), AF.Square, accum_out=st(k, 1)))
        sd_t1.append(T1S.n)
        s_su2 = S.n

        @blk.vector
        def _(vector, s_su2=s_su2):
            vector.wait_ge(S.sem, s_su2)
            vstats(vector, range(2))
        v_st3 = V.n

        @blk.scalar
        def _(scalar, v_st3=v_st3):
            scalar.wait_ge(V.sem, v_st3)
            slnexp(scalar, range(2))
        s_st3 = S.n

        @blk.vector
        def _(vector, j0=j0, s_st3=s_st3, t1t=t1t):
            vector.wait_ge(S.sem, s_st3)
            vnorm(vector, range(2), t1t,
                  lambda k: f_t1all[:, (j0 + k) * C:(j0 + k + 1) * C])
        v_t1 = V.n
        v_t1_list.append(v_t1)



    if _STAGE == 35:
        @blk.gpsimd
        def _(gpsimd):
            gpsimd.wait_ge(T1S.sem, sd_t1[-1])
            gpsimd.wait_ge(V.sem, v_t1_list[-1])
            for j in range(16):
                OUTS.inc(gpsimd.dma_start(out=out[j * 128:(j + 1) * 128, :],
                                          in_=x1d[j * 128:(j + 1) * 128, :]), 16)
            gpsimd.wait_ge(OUTS.sem, OUTS.n)
        bctx.close()
        gctx.close()
        return nc

    v_b2ev = [0, 0]
    p_b2tr = [0, 0]
    for ci in range(8):
        for j in range(16):
            @blk.tensor
            def _(tensor, ci=ci, j=j):
                if ci == 0:
                    tensor.wait_ge(V.sem, v_t1_list[j // 2 + 1])
                tensor.wait_ge(V.sem, v_b2ev[-2])
                _mm = tensor.transpose(
                    (psT0 if j % 2 == 0 else psT1)[:, :],
                    f_t1all[:, j * C + ci * 128:j * C + (ci + 1) * 128],
                    c_identbf[:])
                P.inc(_mm)
            p_b2tr.append(P.n)

            @blk.vector
            def _(vector, ci=ci, j=j, pt=P.n):
                vector.wait_ge(P.sem, pt)
                V.inc(vector.tensor_copy(
                    f_t1T[:, j * 128:(j + 1) * 128],
                    (psT0 if j % 2 == 0 else psT1)[:, :]))
            v_b2ev.append(V.n)

        @blk.tensor
        def _(tensor, ci=ci, vev=V.n):
            tensor.wait_ge(V.sem, vev)
            for tq in range(4):
                _mm = tensor.matmul(psA[tq][0:8, :], c_centg[:, ci * E:(ci + 1) * E],
                                    f_t1T[:, tq * TC:(tq + 1) * TC],
                                    start=(ci == 0), stop=(ci == 7))
            P.inc(_mm)
    p_scm_all = P.n

    @blk.vector
    def _(vector, pp=P.n):
        vector.wait_ge(P.sem, pp)
        for tq in range(4):
            V.inc(vector.tensor_scalar(f_sc[:, tq * TC:(tq + 1) * TC], psA[tq][0:8, :],
                                       c_sb8[:], None, ALU.add))
    v_sc = V.n

    if _STAGE == 45:
        @blk.gpsimd
        def _(gpsimd):
            gpsimd.wait_ge(V.sem, v_sc)
            for j in range(16):
                OUTS.inc(gpsimd.dma_start(out=out[j * 128:(j + 1) * 128, :],
                                          in_=x1d[j * 128:(j + 1) * 128, :]), 16)
            gpsimd.wait_ge(OUTS.sem, OUTS.n)
        bctx.close()
        gctx.close()
        return nc

    @blk.gpsimd
    def _(gpsimd):
        gpsimd.wait_ge(V.sem, v_sc)
        G0.inc(gpsimd.dma_start(out=scores_in, in_=f_sc[:]), 16)
        gpsimd.wait_ge(G0.sem, G0.n)
        CC.inc(gpsimd.collective_compute(
            "AllGather", ALU.bypass, replica_groups=GRP8,
            ins=[scores_in.opt()], outs=[scores_g.opt()]))
    cc_sc = CC.n

    if _STAGE <= 4:
        @blk.gpsimd
        def _(gpsimd):
            gpsimd.wait_ge(CC.sem, cc_sc)
            for j in range(16):
                OUTS.inc(gpsimd.dma_start(out=out[j * 128:(j + 1) * 128, :],
                                          in_=x1d[j * 128:(j + 1) * 128, :]), 16)
            gpsimd.wait_ge(OUTS.sem, OUTS.n)
        bctx.close()
        gctx.close()
        return nc

    @blk.vector
    def _(vector):
        V.inc(vector.memset(f_claim[:], 0.0))
        V.inc(vector.memset(f_rall[:], 0.0))
    v_ffninit = V.n

    gd_scatter = []
    cc_routed = []
    p_trm_list = [0]
    regbox = {}

    @blk.vector
    def _(vector):
        vector.wait_ge(V.sem, v_ffninit)
        V.inc(vector.memset(f_db[:], 0.0))
    v_zfill = V.n

    @blk.gpsimd
    def _(gpsimd):
        regbox["bnd"] = gpsimd.alloc_register(name="bndreg")
        gpsimd.reg_mov(regbox["bnd"], E * PAD - 1)
        gpsimd.wait_ge(V.sem, v_zfill)
        nrow = E * PAD + 64
        for r0 in range(0, nrow, 128):
            SCT.inc(gpsimd.dma_start(out=routed_loc[r0:min(r0 + 128, nrow), :],
                                     in_=f_db[0:min(128, nrow - r0), 0:C]), 16)

    sct_zfill = SCT.n
    crw_list = [0]
    v_sel_list = []
    for e in range(E):
        @blk.sync
        def _(sync, e=e):
            sync.wait_ge(CC.sem, cc_sc)
            if e > 0:
                sync.wait_ge(V.sem, v_sel_list[-1])
            ZL.inc(sync.dma_start(
                out=f_Z[:],
                in_=scores_g.rearrange("(c e) (pr f) -> e c pr f", e=8, f=128)[e].opt()), 16)
        ld_ze = ZL.n

        @blk.vector
        def _(vector, e=e, ld_ze=ld_ze):
            vector.wait_ge(ZL.sem, ld_ze)
            if e == 0:
                vector.wait_ge(V.sem, v_ffninit)
            w = V.inc(vector.scalar_tensor_tensor(f_Z[:], f_claim[:], NEG, f_Z[:],
                                                  ALU.mult, ALU.add))
            V.inc(vector.memset(f_loC[:], 0.0))
            vector.wait_ge(V.sem, w + 1)
            V.inc(vector.tensor_scalar(f_dmp[:], f_Z[:], f_loC[:], 0.0,
                                       ALU.is_gt, ALU.add, accum_out=f_rowc[:]))
        v_cmp = V.n
        for it in range(BIS):
            d = 64.0 * (0.5 ** it)

            @blk.tensor
            def _(tensor, v_cmp=v_cmp):
                tensor.wait_ge(V.sem, v_cmp)
                _mm = tensor.matmul(psB[0][:, 0:1], c_ones128[:], f_rowc[:],
                              start=True, stop=True)
                P.inc(_mm)
            p_cn = P.n

            @blk.vector
            def _(vector, it=it, d=d, p_cn=p_cn, last=(it == BIS - 1)):
                vector.wait_ge(P.sem, p_cn)
                w = V.inc(vector.tensor_scalar(f_step[:], psB[0][:, 0:1], 2048.0, d,
                                               ALU.is_ge, ALU.mult))
                vector.wait_ge(V.sem, w)
                w = V.inc(vector.scalar_tensor_tensor(f_loC[:], f_step[:], -d / 2,
                                                      f_loC[:], ALU.add, ALU.add))
                vector.wait_ge(V.sem, w)
                if last:
                    w = V.inc(vector.tensor_scalar(f_self[:], f_Z[:], f_loC[:], 0.0,
                                                   ALU.is_gt, ALU.add,
                                                   accum_out=f_rowc[:]))
                    vector.wait_ge(V.sem, w)
                    V.inc(vector.tensor_max(f_claim[:], f_claim[:], f_self[:]))
                else:
                    V.inc(vector.tensor_scalar(f_dmp[:], f_Z[:], f_loC[:], 0.0,
                                               ALU.is_gt, ALU.add,
                                               accum_out=f_rowc[:]))
            v_cmp = V.n
        v_sel = V.n
        v_sel_list.append(v_sel)

        @blk.tensor
        def _(tensor, v_sel=v_sel):
            tensor.wait_ge(V.sem, v_sel)
            _mm = tensor.matmul(psB[0][0:8, 2:3], c_blockind[:], f_rowc[:], start=True, stop=True)
            _mm = tensor.matmul(psB[0][:, 3:4], c_ltri[:], f_rowc[:], start=True, stop=True)
            P.inc(_mm)
        p_cnt = P.n

        @blk.vector
        def _(vector, e=e, p_cnt=p_cnt):
            vector.wait_ge(P.sem, p_cnt)
            V.inc(vector.tensor_copy(f_cnt8[:, e:e + 1], psB[0][0:8, 2:3]))
        v_cnt8 = V.n

        @blk.tensor
        def _(tensor, e=e, v_cnt8=v_cnt8):
            tensor.wait_ge(V.sem, v_cnt8)
            _mm = tensor.matmul(psB[0][0:1, 4:12], f_cnt8[:, e:e + 1], c_ident16[0:8, 0:8],
                          start=True, stop=True)
            _mm = tensor.matmul(psB[0][0:1, 12:13], f_cnt8[:, e:e + 1], c_cmask[:],
                          start=True, stop=True)
            P.inc(_mm)
        p_tr8 = P.n

        @blk.vector
        def _(vector, e=e, p_tr8=p_tr8):
            vector.wait_ge(P.sem, p_tr8)
            vector.wait_ge(CRW.sem, crw_list[-1])
            w = V.inc(vector.tensor_copy(f_crow[0:1, 0:8], psB[0][0:1, 4:12]))
            vector.wait_ge(V.sem, w)
            for sh in (1, 2, 4):
                w = V.inc(vector.tensor_copy(f_crow[0:1, 16:24], f_crow[0:1, 0:8]))
                vector.wait_ge(V.sem, w)
                w = V.inc(vector.tensor_add(f_crow[0:1, sh:8], f_crow[0:1, sh:8],
                                            f_crow[0:1, 16:24 - sh]))
                vector.wait_ge(V.sem, w)
            w = V.inc(vector.tensor_scalar(f_crow[0:1, 8:16], psB[0][0:1, 4:12], -1.0,
                                           float(PAD), ALU.mult, ALU.add))
            vector.wait_ge(V.sem, w)
            V.inc(vector.tensor_copy(f_cc[:], psB[0][0:1, 12:13]))
        v_crow = V.n

        @blk.gpsimd
        def _(gpsimd, e=e, v_crow=v_crow):
            gpsimd.wait_ge(V.sem, v_crow)
            CRW.inc(gpsimd.dma_start(out=crow_bounce[e * 16:(e + 1) * 16, 0:1],
                                    in_=f_crow[0:1, 0:16]), 16)
        crw_list.append(CRW.n)

        @blk.vector
        def _(vector, v_sel=v_sel, p_cnt=p_cnt):
            vector.wait_ge(V.sem, v_sel)
            vector.wait_ge(P.sem, p_cnt)
            w = V.inc(vector.tensor_tensor_scan(
                f_rank[:], c_ones[:, 0:1].to_broadcast([128, 128]), f_self[:],
                0.0, ALU.mult, ALU.add))
            vector.wait_ge(V.sem, w)
            w = V.inc(vector.tensor_sub(f_rank[:], f_rank[:], f_self[:]))
            vector.wait_ge(V.sem, w)
            V.inc(vector.tensor_scalar_add(f_rank2[:], f_rank[:], psB[0][:, 3:4]))
        v_rank = V.n

        @blk.tensor
        def _(tensor, v_crow=v_crow):
            tensor.wait_ge(V.sem, v_crow)
            _mm = tensor.matmul(psB[0][:, 13:14], c_ones1r[:], f_cc[:], start=True, stop=True)
            P.inc(_mm)
        p_cm = P.n

        @blk.vector
        def _(vector, e=e, v_rank=v_rank, p_cm=p_cm):
            vector.wait_ge(V.sem, v_rank)
            vector.wait_ge(P.sem, p_cm)
            vector.wait_ge(P.sem, p_trm_list[-1])
            w = V.inc(vector.tensor_scalar(f_slotm[:, 0:128], f_rank2[:], 1.0,
                                           float(PAD * e) + 1e6, ALU.mult, ALU.add))
            vector.wait_ge(V.sem, w)
            w = V.inc(vector.scalar_tensor_tensor(f_slotm[:, 0:128], f_self[:], -1e6,
                                                  f_slotm[:, 0:128], ALU.mult, ALU.add))
            vector.wait_ge(V.sem, w)
            w = V.inc(vector.tensor_scalar_min(f_slotm[:, 0:128], f_slotm[:, 0:128],
                                               float(E * PAD + 8)))
            w2 = V.inc(vector.tensor_scalar_add(f_slotm[:, 128:256], f_rank2[:],
                                                psB[0][:, 13:14]))
            vector.wait_ge(V.sem, w2)
            for th in (512.0, 4608.0, 8704.0):
                w2 = V.inc(vector.tensor_scalar(f_rank[:], f_slotm[:, 128:256],
                                                th, 3584.0, ALU.is_ge, ALU.mult))
                vector.wait_ge(V.sem, w2)
                w2 = V.inc(vector.tensor_add(f_slotm[:, 128:256],
                                             f_slotm[:, 128:256], f_rank[:]))
                vector.wait_ge(V.sem, w2)
            V.inc(vector.scalar_tensor_tensor(f_slotm[:, 128:256],
                                              f_slotm[:, 128:256], 512.0 * e,
                                              f_self[:], ALU.add, ALU.mult))
        v_slot = V.n

        @blk.tensor
        def _(tensor, v_slot=v_slot):
            tensor.wait_ge(V.sem, v_slot)
            _mm = tensor.matmul(psB[0][:, 16:32], f_slotm[:, 0:128], c_selmat[:],
                          start=True, stop=True)
            _mm = tensor.matmul(psB[0][:, 32:48], f_slotm[:, 128:256], c_selmat[:],
                          start=True, stop=True)
            P.inc(_mm)
        p_trm = P.n
        p_trm_list.append(p_trm)

        @blk.vector
        def _(vector, e=e, p_trm=p_trm):
            vector.wait_ge(P.sem, p_trm)
            if gd_scatter:
                vector.wait_ge(SCT.sem, gd_scatter[-1])
            V.inc(vector.tensor_copy(f_sloti[:], psB[0][:, 16:32]))
            V.inc(vector.tensor_add(f_rall[:], f_rall[:], psB[0][:, 32:48]))
        v_sli = V.n

        @blk.gpsimd
        def _(gpsimd, e=e, v_sli=v_sli):
            gpsimd.wait_ge(V.sem, v_sli)
            gpsimd.wait_ge(V.sem, v_t1_list[-1])
            gpsimd.wait_ge(SCT.sem, sct_zfill)
            for j in range(16):
                SCT.inc(gpsimd.indirect_dma_start(
                    routed_loc, IndirectOffsetOnAxis(f_sloti[:, j:j + 1], 0),
                    f_t1all[:, j * C:(j + 1) * C], None,
                    bounds_check=regbox["bnd"], oob_is_err=False), 16)
        gd_scatter.append(SCT.n)

        @blk.gpsimd
        def _(gpsimd, e=e):
            gpsimd.wait_ge(SCT.sem, gd_scatter[-1])
            CC.inc(gpsimd.collective_compute(
                "AllGather", ALU.bypass, replica_groups=GRP8,
                ins=[routed_loc[e * PAD:(e + 1) * PAD, :].opt()],
                outs=[routed_gall[e * 8 * PAD:(e + 1) * 8 * PAD, :].opt()]))
        cc_routed.append(CC.n)

    @blk.vector
    def _(vector, vv=V.n):
        vector.wait_ge(V.sem, vv)
        w = V.inc(vector.tensor_scalar_min(f_idx2t[:], f_rall[:], float(8 * TH - 1)))
        vector.wait_ge(V.sem, w)
        V.inc(vector.tensor_copy(f_ralli[:], f_idx2t[:]))
    v_ralli = V.n

    @blk.gpsimd
    def _(gpsimd, vv=V.n):
        gpsimd.wait_ge(V.sem, vv)
        gpsimd.wait_ge(CRW.sem, CRW.n)
        CRW.inc(gpsimd.dma_start(out=f_crowcol[:], in_=crow_bounce), 16)
    gd_cm = CRW.n

    @blk.tensor
    def _(tensor, gd_cm=gd_cm):
        tensor.wait_ge(CRW.sem, gd_cm)
        _mm = tensor.matmul(psB[0][0:1, 48:64], f_crowcol[:], c_selmat[:],
                      start=True, stop=True)
        P.inc(_mm)
    p_cmA = P.n

    @blk.vector
    def _(vector, p_cmA=p_cmA):
        vector.wait_ge(P.sem, p_cmA)
        V.inc(vector.tensor_copy(f_cmine[:], psB[0][0:1, 48:64]))
    v_cmine = V.n

    @blk.tensor
    def _(tensor, v_cmine=v_cmine):
        tensor.wait_ge(V.sem, v_cmine)
        _mm = tensor.matmul(psB[0][:, 48:64], c_ones1r[:], f_cmine[:], start=True, stop=True)
        _mm = tensor.matmul(psB[0][:, 64:65], c_ones1r[:], c_expoff[:], start=True, stop=True)
        P.inc(_mm)
    p_cmb = P.n

    @blk.vector
    def _(vector, p_cmb=p_cmb):
        vector.wait_ge(P.sem, p_cmb)
        w = V.inc(vector.tensor_copy(f_cumrep[:, 0:16], psB[0][:, 48:64]))
        V.inc(vector.tensor_copy(f_cumrep[:, 16:17], psB[0][:, 64:65]))
        vector.wait_ge(V.sem, w + 1)
        w = V.inc(vector.tensor_scalar_add(f_idx2f[:], c_stile[:], f_cumrep[:, 16:17]))
        vector.wait_ge(V.sem, w)
        for cb in range(7):
            w = V.inc(vector.tensor_scalar(f_idx2t[:], c_stile[:],
                                           f_cumrep[:, cb:cb + 1],
                                           f_cumrep[:, 8 + cb:9 + cb],
                                           ALU.is_ge, ALU.mult))
            vector.wait_ge(V.sem, w)
            w = V.inc(vector.tensor_add(f_idx2f[:], f_idx2f[:], f_idx2t[:]))
            vector.wait_ge(V.sem, w)
        w = V.inc(vector.tensor_scalar_min(f_idx2t[:], f_idx2f[:],
                                           float(E * 8 * PAD - 1)))
        vector.wait_ge(V.sem, w)
        V.inc(vector.tensor_copy(f_idx2i[:], f_idx2t[:]))
    v_idx2 = V.n

    p_tr_list = [0, 0]
    v_ev_list = [0, 0]
    gd_gr_list = [0]
    for j in range(16):
        @blk.gpsimd
        def _(gpsimd, j=j):
            if j == 0:
                gpsimd.wait_ge(CC.sem, cc_routed[-1])
                gpsimd.wait_ge(V.sem, v_idx2)
            else:
                gpsimd.wait_ge(P.sem, p_tr_list[-1])
            GTH.inc(gpsimd.indirect_dma_start(
                f_grows[:], None,
                routed_gall, IndirectOffsetOnAxis(f_idx2i[:, j:j + 1], 0)), 16)
        gd_gr_list.append(GTH.n)

        for ci in range(8):
            @blk.tensor
            def _(tensor, j=j, ci=ci, gg=gd_gr_list[-1]):
                tensor.wait_ge(GTH.sem, gg)
                tensor.wait_ge(V.sem, v_ev_list[-2])
                _mm = tensor.transpose((psT0 if ci % 2 == 0 else psT1)[:, :],
                                 f_grows[:, ci * 128:(ci + 1) * 128],
                                 c_identbf[:])
                P.inc(_mm)
            p_tr_list.append(P.n)

            @blk.vector
            def _(vector, j=j, ci=ci, pt=P.n):
                vector.wait_ge(P.sem, pt)
                V.inc(vector.tensor_scalar(
                    f_normT[:, ci * TH + j * 128:ci * TH + (j + 1) * 128],
                    (psT0 if ci % 2 == 0 else psT1)[:, :],
                    c_lngb[:, ci:ci + 1], c_lngb[:, 8 + ci:9 + ci],
                    ALU.mult, ALU.add))
            v_ev_list.append(V.n)
    v_normT = V.n

    sd_h = []
    p_h2 = [0]
    p_h1g = [0, 0, 0, 0]
    s_gelu_all = [0, 0]
    p_h2_tq_end = [0]
    v_h2ev = [0, 0]
    v_b1_list = [0, 0]
    def emit_hgather(t):
        @blk.gpsimd
        def _(gpsimd, t=t):
            gpsimd.wait_ge(HS.sem, hs_tq_end[t])
            CC.inc(gpsimd.collective_compute(
                "AllGather", ALU.bypass, replica_groups=GRP8,
                ins=[h_loc[t * 512:(t + 1) * 512, :].opt()],
                outs=[h_g[t * 4096:(t + 1) * 4096, :].opt()]))

    hs_tq_end = []
    p_w2h0 = [0]
    s_gelu_half = {}
    w1sems = [W1L, W1M, W1N, W1P]
    for tq in range(4):
        if tq >= 2:
            emit_hgather(tq - 2)
        for ft in range(32):
            wbuf = (ft % 4) * (8 * 128)

            wsem = w1sems[ft % 4]

            @blk.gpsimd
            def _(gpsimd, tq=tq, ft=ft, wbuf=wbuf, wsem=wsem):
                gpsimd.wait_ge(P.sem, p_h1g[-4])
                if tq == 0 and ft == 0:
                    gpsimd.wait_ge(V.sem, v_normT)
                wsem.inc(gpsimd.dma_start(
                    out=f_w1t[:, wbuf:wbuf + 8 * 128].rearrange("p (a f) -> p a f", a=8),
                    in_=w1.rearrange("(a p) f -> p a f", p=128)[:, :, ft * 128:(ft + 1) * 128]), 16)
            gd_w1 = wsem.n

            @blk.tensor
            def _(tensor, tq=tq, ft=ft, wbuf=wbuf, gd_w1=gd_w1):
                tensor.wait_ge(wsem.sem, gd_w1)
                if tq == 0 and ft == 0:
                    tensor.wait_ge(V.sem, v_normT)
                tensor.wait_ge(S.sem, s_gelu_all[-2])
                bank = psA[ft % 2]
                for ci in range(8):
                    _mm = tensor.matmul(bank[:, :],
                                  f_w1t[:, wbuf + ci * 128:wbuf + (ci + 1) * 128],
                                  f_normT[:, ci * TH + tq * TC:ci * TH + (tq + 1) * TC],
                                  start=(ci == 0), stop=(ci == 7))
                if tq == 0:
                    bc = 65 + (ft % 2)
                    if ft >= 2:
                        tensor.wait_ge(V.sem, v_b1_list[-2])
                    for ci in range(8):
                        _mm = tensor.matmul(psB[1][:, bc:bc + 1],
                                      f_w1t[:, wbuf + ci * 128:wbuf + (ci + 1) * 128],
                                      c_lnbb[:, ci:ci + 1],
                                      start=(ci == 0), stop=(ci == 7))
                P.inc(_mm)
            p_h1 = P.n
            p_h1g.append(p_h1)

            if tq == 0:
                @blk.vector
                def _(vector, ft=ft, p_h1=p_h1):
                    vector.wait_ge(P.sem, p_h1)
                    V.inc(vector.tensor_add(f_b1e[:, ft:ft + 1],
                                            psB[1][:, 65 + (ft % 2):66 + (ft % 2)],
                                            c_b1c[:, ft:ft + 1]))
                v_b1_list.append(V.n)
            v_b1 = V.n

            @blk.scalar
            def _(scalar, tq=tq, ft=ft, p_h1=p_h1, v_b1=v_b1):
                scalar.wait_ge(P.sem, p_h1)
                scalar.wait_ge(V.sem, v_b1)
                scalar.wait_ge(P.sem, p_h2_tq_end[-1])
                S.inc(scalar.activation(f_h[:, ft * TC:(ft + 1) * TC],
                                        psA[ft % 2][:, :], AF.Gelu_apprx_tanh,
                                        bias=f_b1e[:, ft:ft + 1]))
            s_gelu_all.append(S.n)
            if ft == 15:
                s_gelu_half[tq] = S.n

        s_gelu_end = S.n
        for do in range(2):
            @blk.gpsimd
            def _(gpsimd, do=do):
                gpsimd.wait_ge(P.sem, p_w2h0[-1])
                W2L.inc(gpsimd.dma_start(
                    out=f_w2r[:, 0:16 * TC].rearrange("p (a f) -> p a f", a=16),
                    in_=w2.rearrange("(a p) f -> p a f", p=128)[:, 0:16, do * TC:(do + 1) * TC]), 16)
            gd_w2h0 = W2L.n

            @blk.gpsimd
            def _(gpsimd, do=do, pp=p_h2[-1]):
                gpsimd.wait_ge(P.sem, pp)
                W2L.inc(gpsimd.dma_start(
                    out=f_w2r[:, 16 * TC:32 * TC].rearrange("p (a f) -> p a f", a=16),
                    in_=w2.rearrange("(a p) f -> p a f", p=128)[:, 16:32, do * TC:(do + 1) * TC]), 16)
            gd_w2h1 = W2L.n

            for tsub in range(4):
                @blk.tensor
                def _(tensor, tsub=tsub, gd_w2h0=gd_w2h0, gd_w2h1=gd_w2h1,
                      sg0=s_gelu_half[tq], sg1=s_gelu_end):
                    tensor.wait_ge(W2L.sem, gd_w2h0)
                    tensor.wait_ge(S.sem, sg0)
                    tensor.wait_ge(V.sem, v_h2ev[-2])
                    bank = psB[tsub % 2]
                    for ft in range(16):
                        _mm = tensor.matmul(bank[:, :],
                                      f_h[:, ft * TC + tsub * 128:ft * TC + (tsub + 1) * 128],
                                      f_w2r[:, ft * TC:(ft + 1) * TC],
                                      start=(ft == 0), stop=False)
                    if tsub == 3:
                        P.inc(_mm)
                        p_w2h0.append(P.n)
                    tensor.wait_ge(W2L.sem, gd_w2h1)
                    tensor.wait_ge(S.sem, sg1)
                    for ft in range(16, 32):
                        _mm = tensor.matmul(bank[:, :],
                                      f_h[:, ft * TC + tsub * 128:ft * TC + (tsub + 1) * 128],
                                      f_w2r[:, ft * TC:(ft + 1) * TC],
                                      start=False, stop=(ft == 31))
                    P.inc(_mm)
                p_h2.append(P.n)

                @blk.vector
                def _(vector, do=do, tsub=tsub, pp=P.n):
                    vector.wait_ge(P.sem, pp)
                    if sd_h:
                        vector.wait_ge(HS.sem, sd_h[-1])
                    V.inc(vector.tensor_add(f_hrow[:, do * TC:(do + 1) * TC],
                                            psB[tsub % 2][:, :],
                                            c_b2rep[:, do * TC:(do + 1) * TC]))
                v_h2ev.append(V.n)

                @blk.scalar
                def _(scalar, tq=tq, do=do, tsub=tsub, vv=V.n):
                    scalar.wait_ge(V.sem, vv)
                    row = tq * TC + tsub * 128
                    HS.inc(scalar.dma_start(
                        out=h_loc[row:row + 128, do * TC:(do + 1) * TC],
                        in_=f_hrow[:, do * TC:(do + 1) * TC]), 16)
                sd_h.append(HS.n)
        p_h2_tq_end.append(P.n)
        hs_tq_end.append(HS.n)

    emit_hgather(2)
    emit_hgather(3)
    cc_h = CC.n

    def ex1(k):
        return f_x1g[:, k * C:(k + 1) * C]

    def et1(k):
        return f_x1g[:, (2 + k) * C:(3 + k) * C]

    def ehm(k):
        return f_hmg[:, k * C:(k + 1) * C]

    def effo(k):
        return f_t1bg[:, k * C:(k + 1) * C]

    ld_fin = []
    v_fgrp = []
    v_f1g = []
    for g in range(8):
        j0 = 2 * g

        @blk.sync
        def _(sync, g=g, j0=j0):
            if g == 0:
                sync.wait_ge(T1S.sem, sd_t1[-1])
                sync.wait_ge(V.sem, v_t1_list[-1])
            else:
                sync.wait_ge(V.sem, v_fgrp[-1])
            for k in range(2):
                FXL.inc(sync.dma_start(out=ex1(k),
                                       in_=x1d[(j0 + k) * 128:(j0 + k + 1) * 128, :]), 16)
                T1L.inc(sync.dma_start(out=et1(k),
                                       in_=t1d[(j0 + k) * 128:(j0 + k + 1) * 128, :]), 16)
        ld_x1f = FXL.n
        ld_t1f = T1L.n

        @blk.gpsimd
        def _(gpsimd, g=g, j0=j0):
            gpsimd.wait_ge(CC.sem, cc_h)
            gpsimd.wait_ge(V.sem, v_ralli)
            if g > 0:
                gpsimd.wait_ge(V.sem, v_f1g[-1])
            for k in range(2):
                HM.inc(gpsimd.indirect_dma_start(
                    ehm(k), None,
                    h_g, IndirectOffsetOnAxis(f_ralli[:, j0 + k:j0 + k + 1], 0)), 16)
        gd_hm = HM.n

        @blk.vector
        def _(vector, g=g, ld_x1f=ld_x1f, ld_t1f=ld_t1f, gd_hm=gd_hm):
            vector.wait_ge(FXL.sem, ld_x1f)
            vector.wait_ge(T1L.sem, ld_t1f)
            vector.wait_ge(HM.sem, gd_hm)
            vector.wait_ge(OUTS.sem, ld_fin[-1] if ld_fin else 0)
            for k in range(2):
                V.inc(vector.tensor_add(effo(k), et1(k), ehm(k)))
            vector.wait_ge(V.sem, V.n)
            for k in range(2):
                V.inc(vector.tensor_scalar(f_db[:, k * C:(k + 1) * C], effo(k), 0.0, 0.0, ALU.add, ALU.add,
                                           accum_out=st(k, 0)))
        v_f1 = V.n
        v_f1g.append(v_f1)

        @blk.scalar
        def _(scalar, v_f1=v_f1):
            scalar.wait_ge(V.sem, v_f1)
            for k in range(2):
                S.inc(scalar.activation(f_db[:, k * C:(k + 1) * C], effo(k), AF.Square, accum_out=st(k, 1)))
        s_f1 = S.n

        @blk.vector
        def _(vector, s_f1=s_f1):
            vector.wait_ge(S.sem, s_f1)
            vstats(vector, range(2))
        v_f2 = V.n

        @blk.scalar
        def _(scalar, v_f2=v_f2):
            scalar.wait_ge(V.sem, v_f2)
            slnexp(scalar, range(2))
        s_f2 = S.n

        @blk.vector
        def _(vector, g=g, s_f2=s_f2):
            vector.wait_ge(S.sem, s_f2)
            vnorm(vector, range(2), effo, effo)
            vector.wait_ge(V.sem, V.n)
            for k in range(2):
                V.inc(vector.tensor_add(effo(k), effo(k), ex1(k)))
        v_f3 = V.n
        v_fgrp.append(v_f3)

        @blk.sync
        def _(sync, j0=j0, v_f3=v_f3):
            sync.wait_ge(V.sem, v_f3)
            for k in range(2):
                OUTS.inc(sync.dma_start(out=out[(j0 + k) * 128:(j0 + k + 1) * 128, :],
                                        in_=effo(k)), 16)
        ld_fin.append(OUTS.n)

    @blk.gpsimd
    def _(gpsimd):
        gpsimd.wait_ge(OUTS.sem, ld_fin[-1])

    bctx.close()
    gctx.close()
    return nc


_NC_CACHE = {}


def _get_nc():
    key = _STAGE
    if key not in _NC_CACHE:
        _NC_CACHE[key] = build_nc()
    return _NC_CACHE[key]


def kernel(**inputs):
    x = np.asarray(inputs["L0_attn_x"], np.float32)
    td = np.asarray(inputs["L0_attn_time_decay"], np.float32).reshape(-1)
    tf = np.asarray(inputs["L0_attn_time_first"], np.float32).reshape(-1)
    tmk = np.asarray(inputs["L0_attn_time_mix_k"], np.float32).reshape(-1)
    tmv = np.asarray(inputs["L0_attn_time_mix_v"], np.float32).reshape(-1)
    tmr = np.asarray(inputs["L0_attn_time_mix_r"], np.float32).reshape(-1)
    Wk = np.asarray(inputs["L0_attn_Wk"], np.float32)
    Wv = np.asarray(inputs["L0_attn_Wv"], np.float32)
    Wr = np.asarray(inputs["L0_attn_Wr"], np.float32)
    Wo = np.asarray(inputs["L0_attn_Wo"], np.float32)
    cent = np.asarray(inputs["L0_ffn_centroids"], np.float32)
    lng = np.asarray(inputs["L0_ffn_ln_g"], np.float32)
    lnb = np.asarray(inputs["L0_ffn_ln_b"], np.float32)
    w1 = np.asarray(inputs["L0_ffn_w1"], np.float32)
    b1 = np.asarray(inputs["L0_ffn_b1"], np.float32)
    w2 = np.asarray(inputs["L0_ffn_w2"], np.float32)
    b2 = np.asarray(inputs["L0_ffn_b2"], np.float32)

    WkT = np.ascontiguousarray(Wk.T)
    WvT = np.ascontiguousarray(Wv.T)
    WrT = np.ascontiguousarray(Wr.T)
    WoT = np.ascontiguousarray(Wo.T)
    centT = np.ascontiguousarray(cent.T)

    def tile_col(vec):
        return np.ascontiguousarray(vec.reshape(8, 128).T)

    blockind = np.zeros((128, 8), np.float32)
    for p in range(128):
        blockind[p, p // 16] = 1.0
    ltri = np.zeros((128, 128), np.float32)
    for m in range(128):
        for p in range((m // 16) * 16, m):
            ltri[p, m] = 1.0
    stl = (np.arange(16)[None, :] * 128 + np.arange(128)[:, None]).astype(np.float32)
    ident16 = np.eye(16, dtype=np.float32)
    identbf = np.eye(128, dtype=np.float32).astype(ml_dtypes.bfloat16)

    in_maps = []
    for c in range(8):
        b, h = c // 2, c % 2
        tdh = td[h * DH:(h + 1) * DH]
        tfh = tf[h * DH:(h + 1) * DH]
        tdtf = np.concatenate([
            np.ascontiguousarray(tdh.reshape(4, 128).T),
            np.ascontiguousarray(tfh.reshape(4, 128).T)], axis=1)
        tmix = np.concatenate([tile_col(tmk), tile_col(tmv), tile_col(tmr)], axis=1)
        lngb_ = np.concatenate([tile_col(lng), tile_col(lnb)], axis=1)
        b1c = np.ascontiguousarray(b1[c].reshape(32, 128).T)
        m = {
            "xT": np.ascontiguousarray(x[b].T),
            "xres": np.ascontiguousarray(x[b, h * TH:(h + 1) * TH]),
            "wkT": np.ascontiguousarray(WkT[:, h * DH:(h + 1) * DH]),
            "wvT": np.ascontiguousarray(WvT[:, h * DH:(h + 1) * DH]),
            "wrT": np.ascontiguousarray(WrT[:, h * DH:(h + 1) * DH]),
            "woT": WoT,
            "tdtf": np.ascontiguousarray(tdtf),
            "tmix": np.ascontiguousarray(tmix),
            "lngb": np.ascontiguousarray(lngb_),
            "lnbb": np.ascontiguousarray(tile_col(lnb).astype(ml_dtypes.bfloat16)),
            "centT": centT,
            "w1": np.ascontiguousarray(w1[c]),
            "b1cc": b1c,
            "w2": np.ascontiguousarray(w2[c]),
            "b2row": np.ascontiguousarray(b2[c].reshape(1, C)),
            "blockind": blockind,
            "ltri16": ltri,
            "cmask": (np.arange(8) < c).astype(np.float32).reshape(8, 1),
            "stile": stl,
            "onescol": np.ones((128, 1), np.float32),
            "onescolb": np.ones((128, 1), np.float32).astype(ml_dtypes.bfloat16),
            "onesrow1": np.ones((1, 128), np.float32),
            "ones128": np.ones((128, 128), np.float32),
            "identbf": identbf,
            "ident16": ident16,
            "epscol": np.full((128, 1), EPS, np.float32),
            "wkvidx": (h * 4096 + (np.arange(32)[None, :] * 128)
                       + np.arange(128)[:, None]).astype(np.int32),
            "rowidx": (c * 16 + np.arange(16)[:, None]).astype(np.int32),
                        "selmat": np.eye(128, dtype=np.float32)[:, c * 16:c * 16 + 16].copy(),
            "expoff": np.array([[c * 8 * PAD]], np.float32),
        }
        in_maps.append(m)

    nc = _get_nc()
    res = run_bass_kernel_spmd(nc, in_maps, core_ids=list(range(8)))
    outp = np.zeros((B, T, C), np.float32)
    for c in range(8):
        b, h = c // 2, c % 2
        outp[b, h * TH:(h + 1) * TH] = res.results[c]["out"]
    return outp


# version 18 (final)
# speedup vs baseline: 1.0095x; 1.0095x over previous
import numpy as np
import ml_dtypes
import concourse.bass as bass
import concourse.mybir as mybir
from concourse import dve_ops
from concourse.bass_utils import run_bass_kernel_spmd
from concourse.bass_core_rust import IndirectOffsetOnAxis
from contextlib import ExitStack

F32 = mybir.dt.float32
BF16 = mybir.dt.bfloat16
I32 = mybir.dt.int32
AF = mybir.ActivationFunctionType
ALU = mybir.AluOpType

B, T, C, E, FF = 4, 4096, 1024, 8, 4096
TC = 512
NCH = T // TC
DH = 512
TH = 2048
PAD = 384
NEG = -1.0e38
EPS = 1e-5
BIS = 28
GRP8 = [[0, 1, 2, 3, 4, 5, 6, 7]]
PAIRS = [[0, 1], [2, 3], [4, 5], [6, 7]]
_STAGE = 8


class Ctr:
    def __init__(self, sem):
        self.sem = sem
        self.n = 0

    def inc(self, ins, k=1):
        ins.then_inc(self.sem, k)
        self.n += k
        return self.n


def build_nc():
    nc = bass.Bass("TRN2", target_bir_lowering=False)

    def din(name, shape, dtype=F32):
        return nc.dram_tensor(name, shape, dtype, kind="ExternalInput").ap()

    def dint(name, shape, dtype=F32):
        return nc.dram_tensor(name, shape, dtype, kind="Internal").ap()

    xT = din("xT", [C, T])
    xres = din("xres", [TH, C])
    wkT = din("wkT", [C, DH]); wvT = din("wvT", [C, DH]); wrT = din("wrT", [C, DH])
    woT = din("woT", [C, C])
    tdtf = din("tdtf", [128, 8])
    tmix = din("tmix", [128, 24])
    lngb = din("lngb", [128, 16])
    lnbb = din("lnbb", [128, 8], BF16)
    centT = din("centT", [C, E])
    w1 = din("w1", [C, FF]); b1cc = din("b1cc", [128, 32])
    w2 = din("w2", [FF, C]); b2row = din("b2row", [1, C])
    blockind = din("blockind", [128, 8])
    ltri16 = din("ltri16", [128, 128])
    cmask = din("cmask", [8, 1])
    stile = din("stile", [128, 16])
    onescol = din("onescol", [128, 1])
    onescolb = din("onescolb", [128, 1], BF16)
    onesrow1 = din("onesrow1", [1, 128])
    ones128 = din("ones128", [128, 128])
    identbf = din("identbf", [128, 128], BF16)
    ident16 = din("ident16", [16, 16])
    epscol = din("epscol", [128, 1])
    wkvidx = din("wkvidx", [128, 32], I32)
    rowidx = din("rowidx", [16, 1], I32)
    selmat = din("selmat", [128, 16])
    expoff = din("expoff", [1, 1])
    out = nc.dram_tensor("out", [TH, C], F32, kind="ExternalOutput").ap()

    x1d = dint("x1d", [TH, C])
    t1d = dint("t1d", [TH, C])
    wkv_in = dint("wkv_in", [NCH * DH, TC], BF16)
    wkv_all = dint("wkv_all", [NCH * 2 * DH, TC], BF16)
    scores_in = dint("scores_in", [E, TH])
    scores_g = dint("scores_g", [8 * E, TH])
    rk_bounce = dint("rk_bounce", [128, 256])
    crow_bounce = dint("crow_bounce", [128, 1])
    routed_loc = dint("routed_loc", [E * PAD + 64, C], BF16)
    routed_gall = dint("routed_gall", [E * 8 * PAD, C], BF16)
    h_loc = dint("h_loc", [TH, C], BF16)
    h_g = dint("h_g", [8 * TH, C], BF16)

    gctx = ExitStack()

    def sbg(name, shape, dtype=F32):
        return gctx.enter_context(nc.sbuf_tensor(name, shape, dtype))

    c_ones = sbg("c_ones", [128, 1])
    c_onesb = sbg("c_onesb", [128, 1], BF16)
    c_ones1r = sbg("c_ones1r", [1, 128])
    c_ones128 = sbg("c_ones128", [128, 128])
    c_identbf = sbg("c_identbf", [128, 128], BF16)
    c_ident16 = sbg("c_ident16", [16, 16])
    c_blockind = sbg("c_blockind", [128, 8])
    c_ltri = sbg("c_ltri", [128, 128])
    c_cmask = sbg("c_cmask", [8, 1])
    c_stile = sbg("c_stile", [128, 16])
    c_tdtf = sbg("c_tdtf", [128, 8])
    c_w = sbg("c_w", [128, 4])
    c_u = sbg("c_u", [128, 4])
    c_tmix = sbg("c_tmix", [128, 24])
    c_lngb = sbg("c_lngb", [128, 16])
    c_lnbb = sbg("c_lnbb", [128, 8], BF16)
    c_centg = sbg("c_centg", [128, 8 * E], BF16)
    c_sb8 = sbg("c_sb8", [8, 1])
    c_b2rep = sbg("c_b2rep", [128, C])
    c_b1c = sbg("c_b1c", [128, 32])
    c_eps = sbg("c_eps", [128, 1])
    c_wkvidx = sbg("c_wkvidx", [128, 32], I32)
    c_rowidx = sbg("c_rowidx", [16, 1], I32)
    c_selmat = sbg("c_selmat", [128, 16])
    c_expoff = sbg("c_expoff", [1, 1])

    psA = [gctx.enter_context(nc.psum_tensor(f"psA{i}", [128, TC], F32)) for i in range(4)]
    psB = [gctx.enter_context(nc.psum_tensor(f"psB{i}", [128, TC], F32)) for i in range(2)]
    psT0 = gctx.enter_context(nc.psum_tensor("psT0", [128, 128], BF16))
    psT1 = gctx.enter_context(nc.psum_tensor("psT1", [128, 128], BF16))

    LD = Ctr(gctx.enter_context(nc.semaphore("LD")))
    SD = Ctr(gctx.enter_context(nc.semaphore("SD")))
    GD = Ctr(gctx.enter_context(nc.semaphore("GD")))
    CC = Ctr(gctx.enter_context(nc.semaphore("CC")))
    V = Ctr(gctx.enter_context(nc.semaphore("V")))
    S = Ctr(gctx.enter_context(nc.semaphore("S")))
    P = Ctr(gctx.enter_context(nc.semaphore("P")))
    G = Ctr(gctx.enter_context(nc.semaphore("G")))
    G0 = Ctr(gctx.enter_context(nc.semaphore("G0")))
    GW = Ctr(gctx.enter_context(nc.semaphore("GW")))
    XL = Ctr(gctx.enter_context(nc.semaphore("XL")))
    WKVL = Ctr(gctx.enter_context(nc.semaphore("WKVL")))
    BNC = Ctr(gctx.enter_context(nc.semaphore("BNC")))
    CRW = Ctr(gctx.enter_context(nc.semaphore("CRW")))
    SCT = Ctr(gctx.enter_context(nc.semaphore("SCT")))
    GTH = Ctr(gctx.enter_context(nc.semaphore("GTH")))
    MIX = Ctr(gctx.enter_context(nc.semaphore("MIX")))
    W1L = Ctr(gctx.enter_context(nc.semaphore("W1L")))
    W1M = Ctr(gctx.enter_context(nc.semaphore("W1M")))
    W1N = Ctr(gctx.enter_context(nc.semaphore("W1N")))
    W1P = Ctr(gctx.enter_context(nc.semaphore("W1P")))
    W2L = Ctr(gctx.enter_context(nc.semaphore("W2L")))
    W2H = Ctr(gctx.enter_context(nc.semaphore("W2H")))
    HM = Ctr(gctx.enter_context(nc.semaphore("HM")))
    XR = Ctr(gctx.enter_context(nc.semaphore("XR")))
    X1L = Ctr(gctx.enter_context(nc.semaphore("X1L")))
    TTL = Ctr(gctx.enter_context(nc.semaphore("TTL")))
    ZL = Ctr(gctx.enter_context(nc.semaphore("ZL")))
    FXL = Ctr(gctx.enter_context(nc.semaphore("FXL")))
    OUTS = Ctr(gctx.enter_context(nc.semaphore("OUTS")))
    WKS = Ctr(gctx.enter_context(nc.semaphore("WKS")))
    X1S = Ctr(gctx.enter_context(nc.semaphore("X1S")))
    T1S = Ctr(gctx.enter_context(nc.semaphore("T1S")))
    T1L = Ctr(gctx.enter_context(nc.semaphore("T1L")))
    T2S = Ctr(gctx.enter_context(nc.semaphore("T2S")))
    HS = Ctr(gctx.enter_context(nc.semaphore("HS")))

    blk = gctx.enter_context(nc.Block())

    @blk.sync
    def _(sync):
        for dst, src in [(c_ones, onescol), (c_onesb, onescolb),
                         (c_ones1r, onesrow1), (c_ones128, ones128),
                         (c_identbf, identbf),
                         (c_ident16, ident16), (c_blockind, blockind),
                         (c_ltri, ltri16), (c_cmask, cmask), (c_stile, stile),
                         (c_tdtf, tdtf), (c_tmix, tmix), (c_lngb, lngb),
                         (c_lnbb, lnbb), (c_b1c, b1cc), (c_eps, epscol), (c_wkvidx, wkvidx), (c_rowidx, rowidx),
                         (c_selmat, selmat), (c_expoff, expoff)]:
            LD.inc(sync.dma_start(out=dst[:], in_=src), 16)

    @blk.gpsimd
    def _(gpsimd):
        G0.inc(gpsimd.dma_start(
            out=c_centg[:].rearrange("p (a f) -> p a f", a=8),
            in_=centT.rearrange("(a p) f -> p a f", p=128)), 16)
        G0.inc(gpsimd.dma_start(out=c_b2rep[0:1, :], in_=b2row), 16)
    ld0, gd0 = LD.n, G0.n

    @blk.tensor
    def _(tensor):
        tensor.wait_ge(LD.sem, ld0)
        tensor.wait_ge(G0.sem, gd0)
        for ci in range(8):
            _mm = tensor.matmul(psB[0][0:8, 0:1], c_centg[:, ci * E:(ci + 1) * E],
                          c_lnbb[:, ci:ci + 1], start=(ci == 0), stop=(ci == 7))
        for j in range(2):
            _mm = tensor.matmul(psA[j][:, :], c_ones1r[:], c_b2rep[0:1, j * TC:(j + 1) * TC],
                          start=True, stop=True)
        P.inc(_mm)
    p_const = P.n

    @blk.vector
    def _(vector):
        vector.wait_ge(P.sem, p_const)
        V.inc(vector.tensor_copy(c_sb8[:], psB[0][0:8, 0:1]))
        V.inc(vector.tensor_copy(c_b2rep[:, 0:TC], psA[0][:, :]))
        w = V.inc(vector.tensor_copy(c_b2rep[:, TC:C], psA[1][:, :]))
        vector.wait_ge(V.sem, w)
        for ci in range(8):
            V.inc(vector.tensor_scalar_mul(c_centg[:, ci * E:(ci + 1) * E],
                                           c_centg[:, ci * E:(ci + 1) * E],
                                           c_lngb[:, ci:ci + 1]))
        V.inc(vector.nop())
    v_const = V.n

    @blk.scalar
    def _(scalar):
        scalar.wait_ge(LD.sem, ld0)
        w = S.inc(scalar.activation(c_w[:], c_tdtf[:, 0:4], AF.Exp))
        scalar.wait_ge(S.sem, w)
        w = S.inc(scalar.mul(c_w[:], c_w[:], -1.0))
        S.inc(scalar.copy(c_u[:], c_tdtf[:, 4:8]))
    s_const = S.n

    actx = ExitStack()

    def sba(name, shape, dtype=F32):
        return actx.enter_context(nc.sbuf_tensor(name, shape, dtype))

    c_wkTs = sba("c_wkTs", [128, 8 * DH], BF16)
    c_wvTs = sba("c_wvTs", [128, 8 * DH], BF16)
    c_wrTs = sba("c_wrTs", [128, 8 * DH], BF16)
    c_woTs = sba("c_woTs", [128, 8 * C], BF16)
    a_xb = sba("a_xb", [128, 8 * TC], BF16)
    a_xh = sba("a_xh", [128, 8 * (TC + 1)], BF16)
    a_hc = sba("a_hc", [128, 8], BF16)
    a_dall = sba("a_dall", [128, 8 * TC], BF16)
    a_mx = sba("a_mx", [128, 2 * 8 * TC], BF16)
    a_rows = sba("a_rows", [1, 4 * TC])
    a_repm = sba("a_repm", [128, TC], BF16)
    a_repr = sba("a_repr", [128, TC], BF16)
    a_k = sba("a_k", [128, 4 * TC])
    a_v = sba("a_v", [128, 4 * TC])
    a_sr = sba("a_sr", [128, 4 * TC], BF16)
    a_p = sba("a_p", [128, 4 * (TC + 1)])
    a_aa = sba("a_aa", [128, 4 * (TC + 1)])
    a_bb = sba("a_bb", [128, 4 * (TC + 1)])
    a_pst = sba("a_pst", [128, 12])
    a_s1 = sba("a_s1", [128, 4 * TC]); a_s2 = sba("a_s2", [128, 4 * TC])
    a_s3 = sba("a_s3", [128, 4 * TC]); a_s4 = sba("a_s4", [128, 4 * TC])
    a_s5 = sba("a_s5", [128, 4 * TC])
    a_srw = sba("a_srw", [128, 4 * TC], BF16)
    a_wkvf = sba("a_wkvf", [128, 8 * TC], BF16)
    a_wout = sba("a_wout", [128, C])
    a_xres = sba("a_xres", [128, C])
    a_x1 = sba("a_x1", [128, C])
    a_st = sba("a_st", [128, 8])

    @blk.gpsimd
    def _(gpsimd):
        for dst, src in [(c_wkTs, wkT), (c_wvTs, wvT), (c_wrTs, wrT)]:
            GW.inc(gpsimd.dma_start(
                out=dst[:].rearrange("p (a f) -> p a f", a=8),
                in_=src.rearrange("(a p) f -> p a f", p=128)), 16)
        GW.inc(gpsimd.dma_start(
            out=c_woTs[:].rearrange("p (a f) -> p a f", a=8),
            in_=woT.rearrange("(a p) f -> p a f", p=128)), 16)
    gd_w = GW.n

    @blk.vector
    def _(vector):
        V.inc(vector.memset(a_pst[:, 0:4], NEG))
        V.inc(vector.memset(a_pst[:, 4:12], 0.0))
        V.inc(vector.memset(a_hc[:], 0.0))

    sd_wkv = []
    cc_wkv = []
    p_mix_done = [P.n]

    def emit_ccwkv(t):
        @blk.gpsimd
        def _(gpsimd, t=t):
            gpsimd.wait_ge(WKS.sem, sd_wkv[t])
            CC.inc(gpsimd.collective_compute(
                "AllGather", ALU.bypass, replica_groups=PAIRS,
                ins=[wkv_in[t * DH:(t + 1) * DH, :].opt()],
                outs=[wkv_all[t * 2 * DH:(t + 1) * 2 * DH, :].opt()]))
        cc_wkv.append(CC.n)

    v_xh_list = []
    for tch in range(NCH):
        @blk.gpsimd
        def _(gpsimd, tch=tch):
            if tch == 0:
                gpsimd.wait_ge(V.sem, v_const)
                gpsimd.wait_ge(S.sem, s_const)
            else:
                gpsimd.wait_ge(P.sem, p_mix_done[-1])
            XL.inc(gpsimd.dma_start(
                out=a_xb[:].rearrange("p (a f) -> p a f", a=8),
                in_=xT.rearrange("(a p) t -> p a t", p=128)[:, :, tch * TC:(tch + 1) * TC]), 16)
        ld_x = XL.n

        @blk.tensor
        def _(tensor, ld_x=ld_x, tch=tch):
            tensor.wait_ge(XL.sem, ld_x)
            if tch == 0:
                tensor.wait_ge(GW.sem, gd_w)
                tensor.wait_ge(V.sem, v_const)
            for ci in range(8):
                _mm = tensor.matmul(psB[0][0:1, :], c_onesb[:],
                              a_xb[:, ci * TC:(ci + 1) * TC],
                              start=(ci == 0), stop=(ci == 7))
            P.inc(_mm)

        @blk.scalar
        def _(scalar, ld_x=ld_x):
            scalar.wait_ge(XL.sem, ld_x)
            for ci in range(8):
                S.inc(scalar.activation(a_xh[:, ci * (TC + 1) + 1:ci * (TC + 1) + 1 + TC],
                                        a_xb[:, ci * TC:(ci + 1) * TC], AF.Square))
        s_sq = S.n

        @blk.tensor
        def _(tensor, s_sq=s_sq):
            tensor.wait_ge(S.sem, s_sq)
            for ci in range(8):
                _mm = tensor.matmul(psB[1][0:1, :], c_onesb[:],
                              a_xh[:, ci * (TC + 1) + 1:ci * (TC + 1) + 1 + TC],
                              start=(ci == 0), stop=(ci == 7))
            P.inc(_mm)
        p_sq = P.n

        @blk.vector
        def _(vector, p_sq=p_sq):
            vector.wait_ge(P.sem, p_sq)
            mu = a_rows[0:1, 0:TC]; sq = a_rows[0:1, TC:2 * TC]
            var = a_rows[0:1, 2 * TC:3 * TC]
            V.inc(vector.tensor_scalar_mul(mu, psB[0][0:1, :], 1.0 / C))
            w = V.inc(vector.tensor_scalar_mul(sq, psB[1][0:1, :], 1.0 / C))
            vector.wait_ge(V.sem, w)
            w = V.inc(vector.tensor_mul(var, mu, mu))
            vector.wait_ge(V.sem, w)
            V.inc(vector.tensor_sub(var, sq, var))
        v_var = V.n

        @blk.scalar
        def _(scalar, v_var=v_var):
            scalar.wait_ge(V.sem, v_var)
            var = a_rows[0:1, 2 * TC:3 * TC]; rst = a_rows[0:1, TC:2 * TC]
            lnv = a_rows[0:1, 3 * TC:4 * TC]
            w = S.inc(scalar.activation(lnv, var, AF.Ln, bias=c_eps[0:1, 0:1]))
            scalar.wait_ge(S.sem, w)
            S.inc(scalar.activation(rst, lnv, AF.Exp, scale=-0.5))
        s_rst = S.n

        @blk.vector
        def _(vector, s_rst=s_rst):
            vector.wait_ge(S.sem, s_rst)
            mu = a_rows[0:1, 0:TC]; rst = a_rows[0:1, TC:2 * TC]
            nmr = a_rows[0:1, 3 * TC:4 * TC]
            w = V.inc(vector.tensor_mul(nmr, mu, rst))
            vector.wait_ge(V.sem, w)
            V.inc(vector.tensor_scalar_mul(nmr, nmr, -1.0))
        v_nmr = V.n

        @blk.tensor
        def _(tensor, v_nmr=v_nmr):
            tensor.wait_ge(V.sem, v_nmr)
            _mm = tensor.matmul(psB[0][:, :], c_ones1r[:], a_rows[0:1, 3 * TC:4 * TC],
                          start=True, stop=True)
            _mm = tensor.matmul(psB[1][:, :], c_ones1r[:], a_rows[0:1, TC:2 * TC],
                          start=True, stop=True)
            P.inc(_mm)
        p_bc = P.n

        @blk.vector
        def _(vector, p_bc=p_bc):
            vector.wait_ge(P.sem, p_bc)
            V.inc(vector.tensor_copy(a_repm[:], psB[0][:, :]))
            w = V.inc(vector.tensor_copy(a_repr[:], psB[1][:, :]))
            vector.wait_ge(V.sem, w)
            for ci in range(8):
                xh = a_xh[:, ci * (TC + 1) + 1:ci * (TC + 1) + 1 + TC]
                w = V.inc(vector.tensor_add(xh, a_xb[:, ci * TC:(ci + 1) * TC], a_repm[:]))
                vector.wait_ge(V.sem, w)
                V.inc(vector.tensor_mul(xh, xh, a_repr[:]))
            for ci in range(8):
                V.inc(vector.tensor_copy(a_xh[:, ci * (TC + 1):ci * (TC + 1) + 1],
                                         a_hc[:, ci:ci + 1]))
            w = V.inc(vector.nop())
            vector.wait_ge(V.sem, w)
            for ci in range(8):
                V.inc(vector.tensor_sub(a_dall[:, ci * TC:(ci + 1) * TC],
                                        a_xh[:, ci * (TC + 1) + 1:ci * (TC + 1) + 1 + TC],
                                        a_xh[:, ci * (TC + 1):ci * (TC + 1) + TC]))
            w = V.inc(vector.nop())
            vector.wait_ge(V.sem, w)
            for ci in range(8):
                V.inc(vector.tensor_copy(a_hc[:, ci:ci + 1],
                                         a_xh[:, ci * (TC + 1) + TC:ci * (TC + 1) + TC + 1]))
        v_xh = V.n
        v_xh_list.append(v_xh)

        for pi, (tmo, wsb) in enumerate([(0, c_wkTs), (8, c_wvTs), (16, c_wrTs)]):
            mxo = ((tch * 3 + pi) % 2) * (8 * TC)

            @blk.vector
            def _(vector, tmo=tmo, mxo=mxo, v_xh=v_xh):
                vector.wait_ge(V.sem, v_xh)
                vector.wait_ge(P.sem, p_mix_done[max(0, len(p_mix_done) - 2)])
                for ci in range(8):
                    V.inc(vector.scalar_tensor_tensor(
                        a_mx[:, mxo + ci * TC:mxo + (ci + 1) * TC],
                        a_dall[:, ci * TC:(ci + 1) * TC],
                        c_tmix[:, tmo + ci:tmo + ci + 1],
                        a_xh[:, ci * (TC + 1):ci * (TC + 1) + TC],
                        ALU.mult, ALU.add))
            v_mx = V.n

            @blk.tensor
            def _(tensor, mxo=mxo, v_mx=v_mx, wsb=wsb):
                tensor.wait_ge(V.sem, v_mx)
                for dt in range(4):
                    for ci in range(8):
                        _mm = tensor.matmul(psA[dt][:, :],
                                      wsb[:, ci * DH + dt * 128:ci * DH + (dt + 1) * 128],
                                      a_mx[:, mxo + ci * TC:mxo + (ci + 1) * TC],
                                      start=(ci == 0), stop=(ci == 7))
                P.inc(_mm)
            p_mix_done.append(P.n)

            if pi < 2:
                dst = a_k if pi == 0 else a_v

                @blk.vector
                def _(vector, dst=dst, pp=P.n):
                    vector.wait_ge(P.sem, pp)
                    for dt in range(4):
                        V.inc(vector.tensor_copy(dst[:, dt * TC:(dt + 1) * TC],
                                                 psA[dt][:, :]))
            else:
                @blk.scalar
                def _(scalar, pp=P.n):
                    scalar.wait_ge(P.sem, pp)
                    for dt in range(4):
                        S.inc(scalar.activation(a_sr[:, dt * TC:(dt + 1) * TC],
                                                psA[dt][:, :], AF.Sigmoid))
        v_kv = V.n
        s_sr = S.n

        if tch >= 1:
            emit_ccwkv(tch - 1)

        def sl(dt):
            return slice(dt * TC, (dt + 1) * TC)

        def pf(dt, a=0, b=0):
            return slice(dt * (TC + 1) + a, dt * (TC + 1) + b)

        @blk.vector
        def _(vector, v_kv=v_kv):
            vector.wait_ge(V.sem, v_kv)
            for dt in range(4):
                V.inc(vector.tensor_copy(a_p[:, pf(dt, 0, 1)], a_pst[:, dt:dt + 1]))
            vector.wait_ge(V.sem, V.n)
            for dt in range(4):
                V.inc(vector.tensor_tensor_scan(
                    a_p[:, pf(dt, 1, TC + 1)],
                    c_w[:, dt:dt + 1].to_broadcast([128, TC]), a_k[:, sl(dt)],
                    a_pst[:, dt:dt + 1], ALU.add, ALU.max))
            vector.wait_ge(V.sem, V.n)
            for dt in range(4):
                V.inc(vector.tensor_copy(a_pst[:, dt:dt + 1], a_p[:, pf(dt, TC, TC + 1)]))
                V.inc(vector.scalar_tensor_tensor(
                    a_s1[:, sl(dt)], a_p[:, pf(dt, 0, TC)], c_w[:, dt:dt + 1],
                    a_p[:, pf(dt, 1, TC + 1)], ALU.add, ALU.subtract))
                V.inc(vector.tensor_sub(a_s2[:, sl(dt)], a_k[:, sl(dt)],
                                        a_p[:, pf(dt, 1, TC + 1)]))
                V.inc(vector.scalar_tensor_tensor(
                    a_s5[:, sl(dt)], a_p[:, pf(dt, 0, TC)], c_u[:, dt:dt + 1],
                    a_k[:, sl(dt)], ALU.subtract, ALU.subtract))
        v_pf = V.n

        @blk.scalar
        def _(scalar, v_pf=v_pf):
            scalar.wait_ge(V.sem, v_pf)
            S.inc(scalar.activation(a_s1[:, :], a_s1[:, :], AF.Exp))
            S.inc(scalar.activation(a_s2[:, :], a_s2[:, :], AF.Exp))
            S.inc(scalar.activation(a_s5[:, :], a_s5[:, :], AF.Exp))
        s_exp = S.n

        @blk.vector
        def _(vector, s_exp=s_exp):
            vector.wait_ge(S.sem, s_exp)
            for dt in range(4):
                V.inc(vector.tensor_mul(a_s3[:, sl(dt)], a_s2[:, sl(dt)], a_v[:, sl(dt)]))
                V.inc(vector.tensor_copy(a_aa[:, pf(dt, 0, 1)], a_pst[:, 4 + dt:5 + dt]))
                V.inc(vector.tensor_copy(a_bb[:, pf(dt, 0, 1)], a_pst[:, 8 + dt:9 + dt]))
            vector.wait_ge(V.sem, V.n)
            for dt in range(4):
                V.inc(vector.tensor_tensor_scan(
                    a_aa[:, pf(dt, 1, TC + 1)], a_s1[:, sl(dt)], a_s3[:, sl(dt)],
                    a_pst[:, 4 + dt:5 + dt], ALU.mult, ALU.add))
                V.inc(vector.tensor_tensor_scan(
                    a_bb[:, pf(dt, 1, TC + 1)], a_s1[:, sl(dt)], a_s2[:, sl(dt)],
                    a_pst[:, 8 + dt:9 + dt], ALU.mult, ALU.add))
            vector.wait_ge(V.sem, V.n)
            for dt in range(4):
                V.inc(vector.tensor_copy(a_pst[:, 4 + dt:5 + dt], a_aa[:, pf(dt, TC, TC + 1)]))
                V.inc(vector.tensor_copy(a_pst[:, 8 + dt:9 + dt], a_bb[:, pf(dt, TC, TC + 1)]))
                V.inc(vector.tensor_mul(a_s3[:, sl(dt)], a_s5[:, sl(dt)], a_aa[:, pf(dt, 0, TC)]))
            vector.wait_ge(V.sem, V.n)
            for dt in range(4):
                V.inc(vector.tensor_add(a_s3[:, sl(dt)], a_s3[:, sl(dt)], a_v[:, sl(dt)]))
                V.inc(vector.tensor_mul(a_s4[:, sl(dt)], a_s5[:, sl(dt)], a_bb[:, pf(dt, 0, TC)]))
        v_nd = V.n

        @blk.scalar
        def _(scalar, v_nd=v_nd):
            scalar.wait_ge(V.sem, v_nd)
            w = S.inc(scalar.activation(a_s1[:, :], a_s4[:, :], AF.Ln, bias=1.0))
            scalar.wait_ge(S.sem, w)
            S.inc(scalar.activation(a_s4[:, :], a_s1[:, :], AF.Exp, scale=-1.0))
        s_rd = S.n

        @blk.vector
        def _(vector, s_rd=s_rd, s_sr=s_sr):
            vector.wait_ge(S.sem, s_rd)
            vector.wait_ge(S.sem, s_sr)
            if sd_wkv:
                vector.wait_ge(WKS.sem, sd_wkv[-1])
            w = V.inc(vector.tensor_mul(a_s3[:, :], a_s3[:, :], a_s4[:, :]))
            vector.wait_ge(V.sem, w)
            V.inc(vector.tensor_mul(a_srw[:, :], a_s3[:, :], a_sr[:, :]))
        v_y = V.n

        @blk.scalar
        def _(scalar, tch=tch, v_y=v_y):
            scalar.wait_ge(V.sem, v_y)
            WKS.inc(scalar.dma_start(
                out=wkv_in[tch * DH:(tch + 1) * DH, :].rearrange(
                    "(dt p) t -> p dt t", p=128),
                in_=a_srw[:].rearrange("p (dt t) -> p dt t", dt=4)), 16)
        sd_wkv.append(WKS.n)

    emit_ccwkv(NCH - 1)

    sd_x1 = []
    v_x1_list = [0]
    v_ws_list = []
    p_wo_list = [0]
    for q in range(4):
        @blk.gpsimd
        def _(gpsimd, q=q):
            gpsimd.wait_ge(CC.sem, cc_wkv[-1])
            gpsimd.wait_ge(P.sem, p_wo_list[-1])
            for dt in range(8):
                WKVL.inc(gpsimd.indirect_dma_start(
                    a_wkvf[:, dt * TC:(dt + 1) * TC], None,
                    wkv_all, IndirectOffsetOnAxis(c_wkvidx[:, q * 8 + dt:q * 8 + dt + 1], 0)), 16)
        gd_wkv = WKVL.n

        for sub in range(4):
            j = q * 4 + sub

            @blk.sync
            def _(sync, j=j):
                sync.wait_ge(V.sem, v_x1_list[-1])
                XR.inc(sync.dma_start(out=a_xres[:], in_=xres[j * 128:(j + 1) * 128, :]), 16)
            ld_xr = XR.n

            @blk.tensor
            def _(tensor, sub=sub, gd_wkv=gd_wkv, j=j):
                tensor.wait_ge(WKVL.sem, gd_wkv)
                if v_ws_list:
                    tensor.wait_ge(V.sem, v_ws_list[-1])
                for do in range(2):
                    for dt in range(8):
                        _mm = tensor.matmul(psB[do][:, :],
                                      a_wkvf[:, dt * TC + sub * 128:dt * TC + (sub + 1) * 128],
                                      c_woTs[:, dt * C + do * TC:dt * C + (do + 1) * TC],
                                      start=(dt == 0), stop=(dt == 7))
                P.inc(_mm)
            p_wo = P.n
            p_wo_list.append(p_wo)

            @blk.vector
            def _(vector, p_wo=p_wo):
                vector.wait_ge(P.sem, p_wo)
                V.inc(vector.tensor_copy(a_wout[:, 0:TC], psB[0][:, :]))
                w = V.inc(vector.tensor_copy(a_wout[:, TC:C], psB[1][:, :]))
                vector.wait_ge(V.sem, w)
                V.inc(vector.tensor_scalar(a_s1[:, 0:TC], a_wout[:, 0:TC], 0.0, 0.0,
                                           ALU.add, ALU.add, accum_out=a_st[:, 0:1]))
                V.inc(vector.tensor_scalar(a_s2[:, 0:TC], a_wout[:, TC:C], 0.0, 0.0,
                                           ALU.add, ALU.add, accum_out=a_st[:, 1:2]))
            v_ws = V.n
            v_ws_list.append(v_ws)

            @blk.scalar
            def _(scalar, v_ws=v_ws):
                scalar.wait_ge(V.sem, v_ws)
                S.inc(scalar.activation(a_s3[:, 0:TC], a_wout[:, 0:TC], AF.Square,
                                        accum_out=a_st[:, 2:3]))
                S.inc(scalar.activation(a_s4[:, 0:TC], a_wout[:, TC:C], AF.Square,
                                        accum_out=a_st[:, 3:4]))
            s_wsq = S.n

            @blk.vector
            def _(vector, s_wsq=s_wsq):
                vector.wait_ge(S.sem, s_wsq)
                w = V.inc(vector.tensor_add(a_st[:, 4:5], a_st[:, 0:1], a_st[:, 1:2]))
                V.inc(vector.tensor_add(a_st[:, 5:6], a_st[:, 2:3], a_st[:, 3:4]))
                vector.wait_ge(V.sem, w + 1)
                w = V.inc(vector.tensor_scalar_mul(a_st[:, 4:5], a_st[:, 4:5], 1.0 / C))
                V.inc(vector.tensor_scalar_mul(a_st[:, 5:6], a_st[:, 5:6], 1.0 / C))
                vector.wait_ge(V.sem, w + 1)
                w = V.inc(vector.tensor_mul(a_st[:, 6:7], a_st[:, 4:5], a_st[:, 4:5]))
                vector.wait_ge(V.sem, w)
                V.inc(vector.tensor_sub(a_st[:, 5:6], a_st[:, 5:6], a_st[:, 6:7]))
            v_wst = V.n

            @blk.scalar
            def _(scalar, v_wst=v_wst):
                scalar.wait_ge(V.sem, v_wst)
                w = S.inc(scalar.activation(a_st[:, 6:7], a_st[:, 5:6], AF.Ln, bias=c_eps[:, 0:1]))
                scalar.wait_ge(S.sem, w)
                S.inc(scalar.activation(a_st[:, 7:8], a_st[:, 6:7], AF.Exp, scale=-0.5))
            s_wrst = S.n

            @blk.vector
            def _(vector, s_wrst=s_wrst, ld_xr=ld_xr):
                vector.wait_ge(S.sem, s_wrst)
                vector.wait_ge(XR.sem, ld_xr)
                if sd_x1:
                    vector.wait_ge(X1S.sem, sd_x1[-1])
                w = V.inc(vector.tensor_mul(a_st[:, 6:7], a_st[:, 4:5], a_st[:, 7:8]))
                vector.wait_ge(V.sem, w)
                w = V.inc(vector.tensor_scalar_mul(a_st[:, 6:7], a_st[:, 6:7], -1.0))
                vector.wait_ge(V.sem, w)
                w = V.inc(vector.tensor_scalar(a_x1[:], a_wout[:], a_st[:, 7:8],
                                               a_st[:, 6:7], ALU.mult, ALU.add))
                vector.wait_ge(V.sem, w)
                V.inc(vector.tensor_add(a_x1[:], a_x1[:], a_xres[:]))
            v_x1 = V.n
            v_x1_list.append(v_x1)

            @blk.scalar
            def _(scalar, j=j, v_x1=v_x1):
                scalar.wait_ge(V.sem, v_x1)
                X1S.inc(scalar.dma_start(out=x1d[j * 128:(j + 1) * 128, :], in_=a_x1[:]), 16)
            sd_x1.append(X1S.n)

    marks_A = dict(V=V.n, S=S.n, P=P.n, GD=GD.n, LD=LD.n, SD=SD.n, CC=CC.n,
                   XL=XL.n, WKVL=WKVL.n, XR=XR.n, X1S=X1S.n, WKS=WKS.n, GW=GW.n)
    actx.close()

    if _STAGE <= 3:
        @blk.gpsimd
        def _(gpsimd):
            gpsimd.wait_ge(X1S.sem, sd_x1[-1])
            for j in range(16):
                OUTS.inc(gpsimd.dma_start(out=out[j * 128:(j + 1) * 128, :],
                                          in_=x1d[j * 128:(j + 1) * 128, :]), 16)
            gpsimd.wait_ge(OUTS.sem, OUTS.n)
        gctx.close()
        return nc

    bctx = ExitStack()

    def sbb(name, shape, dtype=F32, **kw):
        return bctx.enter_context(nc.sbuf_tensor(name, shape, dtype, **kw))

    f_x1g = sbb("f_x1g", [128, 4 * C])
    f_t1bg = sbb("f_t1bg", [128, 2 * C])
    f_st = sbb("f_st", [128, 16])
    f_db = sbb("f_db", [128, 2 * C], BF16)
    f_t1T = sbb("f_t1T", [128, TH], BF16, is_transpose_dest=True)
    f_Z = sbb("f_Z", [128, 128])
    f_dmp = sbb("f_dmp", [128, 128], BF16)
    f_claim = sbb("f_claim", [128, 128])
    f_self = sbb("f_self", [128, 128])
    f_rowc = sbb("f_rowc", [128, 1])
    f_loC = sbb("f_loC", [128, 1])
    f_step = sbb("f_step", [128, 1])
    f_lo = sbb("f_lo", [1, BIS + 2])
    f_cc = sbb("f_cc", [1, 1])
    f_rank = sbb("f_rank", [128, 128])
    f_rank2 = sbb("f_rank2", [128, 128])
    f_slotm = sbb("f_slotm", [128, 256])
    f_mrows = sbb("f_mrows", [16, 256])
    f_sloti = sbb("f_sloti", [128, 16], I32)
    f_rall = sbb("f_rall", [128, 16])
    f_ralli = sbb("f_ralli", [128, 16], I32)
    f_cnt8 = sbb("f_cnt8", [8, 8])
    f_crow = sbb("f_crow", [1, 24])
    f_crowcol = sbb("f_crowcol", [128, 1])
    f_cm16 = sbb("f_cm16", [16, 1])
    f_cmine = sbb("f_cmine", [1, 16])
    f_cumrep = sbb("f_cumrep", [128, 17])
    f_idx2f = sbb("f_idx2f", [128, 16])
    f_idx2t = sbb("f_idx2t", [128, 16])
    f_idx2i = sbb("f_idx2i", [128, 16], I32)
    f_grows = sbb("f_grows", [128, C], BF16)
    f_normT = sbb("f_normT", [128, 8 * TH], BF16)
    f_h = sbb("f_h", [128, 32 * TC], BF16)
    f_w1t = sbb("f_w1t", [128, 4 * 8 * 128], BF16)
    f_b1e = sbb("f_b1e", [128, 32])
    f_hrow = sbb("f_hrow", [128, C], BF16)
    f_hmg = sbb("f_hmg", [128, 2 * C], BF16)
    f_t1all = sbb("f_t1all", [128, 16 * C], BF16)
    f_sc = sbb("f_sc", [8, TH], F32)
    f_w2r = sbb("f_w2r", [128, 32 * TC], BF16)

    def bwait(eng):
        eng.wait_ge(V.sem, marks_A["V"])
        eng.wait_ge(S.sem, marks_A["S"])
        eng.wait_ge(P.sem, marks_A["P"])
        eng.wait_ge(LD.sem, marks_A["LD"])
        eng.wait_ge(CC.sem, marks_A["CC"])
        eng.wait_ge(XL.sem, marks_A["XL"])
        eng.wait_ge(WKVL.sem, marks_A["WKVL"])
        eng.wait_ge(XR.sem, marks_A["XR"])
        eng.wait_ge(X1S.sem, marks_A["X1S"])
        eng.wait_ge(WKS.sem, marks_A["WKS"])

    def st(k, slot):
        return f_st[:, k * 8 + slot:k * 8 + slot + 1]

    def vstats(vector, ks):
        w0 = V.n
        for k in ks:
            V.inc(vector.tensor_scalar_mul(st(k, 2), st(k, 0), 1.0 / C))
            V.inc(vector.tensor_scalar_mul(st(k, 3), st(k, 1), 1.0 / C))
        vector.wait_ge(V.sem, V.n)
        for k in ks:
            V.inc(vector.tensor_mul(st(k, 4), st(k, 2), st(k, 2)))
        vector.wait_ge(V.sem, V.n)
        for k in ks:
            V.inc(vector.tensor_sub(st(k, 3), st(k, 3), st(k, 4)))

    def slnexp(scalar, ks):
        for k in ks:
            S.inc(scalar.activation(st(k, 4), st(k, 3), AF.Ln, bias=c_eps[:, 0:1]))
        scalar.wait_ge(S.sem, S.n)
        for k in ks:
            S.inc(scalar.activation(st(k, 5), st(k, 4), AF.Exp, scale=-0.5))

    def vnorm(vector, ks, src_of_k, dst_of_k):
        for k in ks:
            V.inc(vector.tensor_mul(st(k, 6), st(k, 2), st(k, 5)))
        vector.wait_ge(V.sem, V.n)
        for k in ks:
            V.inc(vector.tensor_scalar_mul(st(k, 6), st(k, 6), -1.0))
        vector.wait_ge(V.sem, V.n)
        for k in ks:
            V.inc(vector.tensor_scalar(dst_of_k(k), src_of_k(k),
                                       st(k, 5), st(k, 6), ALU.mult, ALU.add))

    sd_t1 = []
    v_t1_list = [0]
    v_su_list = []
    for g in range(8):
        par = g % 2
        j0 = 2 * g

        def x1t(k, par=par):
            return f_x1g[:, (par * 2 + k) * C:(par * 2 + k + 1) * C]

        def t1t(k):
            return f_t1bg[:, k * C:(k + 1) * C]

        @blk.sync
        def _(sync, g=g, j0=j0, par=par):
            if g == 0:
                bwait(sync)
            if g >= 1:
                sync.wait_ge(V.sem, v_su_list[-1])
            sync.wait_ge(X1S.sem, sd_x1[j0 + 1])
            for k in range(2):
                X1L.inc(sync.dma_start(out=x1t(k),
                                       in_=x1d[(j0 + k) * 128:(j0 + k + 1) * 128, :]), 16)
        ld_j = X1L.n

        @blk.vector
        def _(vector, g=g, ld_j=ld_j, x1t=x1t):
            if g == 0:
                bwait(vector)
            vector.wait_ge(X1L.sem, ld_j)
            for k in range(2):
                V.inc(vector.tensor_scalar(f_db[:, k * C:(k + 1) * C], x1t(k), 0.0, 0.0, ALU.add, ALU.add,
                                           accum_out=st(k, 0)))
        v_su = V.n
        v_su_list.append(v_su)

        @blk.scalar
        def _(scalar, g=g, ld_j=ld_j, v_su=v_su, x1t=x1t):
            if g == 0:
                bwait(scalar)
            scalar.wait_ge(X1L.sem, ld_j)
            scalar.wait_ge(V.sem, v_su)
            for k in range(2):
                S.inc(scalar.activation(f_db[:, k * C:(k + 1) * C], x1t(k), AF.Square, accum_out=st(k, 1)))
        s_su = S.n

        @blk.vector
        def _(vector, s_su=s_su):
            vector.wait_ge(S.sem, s_su)
            vstats(vector, range(2))
        v_st2 = V.n

        @blk.scalar
        def _(scalar, v_st2=v_st2):
            scalar.wait_ge(V.sem, v_st2)
            slnexp(scalar, range(2))
        s_st2 = S.n

        @blk.vector
        def _(vector, s_st2=s_st2, x1t=x1t, t1t=t1t):
            vector.wait_ge(S.sem, s_st2)
            if sd_t1:
                vector.wait_ge(T1S.sem, sd_t1[-1])
            vnorm(vector, range(2), x1t, t1t)
            vector.wait_ge(V.sem, V.n)
            for k in range(2):
                V.inc(vector.tensor_scalar(f_db[:, k * C:(k + 1) * C], t1t(k), 0.0, 0.0, ALU.add, ALU.add,
                                           accum_out=st(k, 0)))
        v_t1a = V.n

        @blk.scalar
        def _(scalar, j0=j0, v_t1a=v_t1a, t1t=t1t):
            scalar.wait_ge(V.sem, v_t1a)
            for k in range(2):
                T1S.inc(scalar.dma_start(out=t1d[(j0 + k) * 128:(j0 + k + 1) * 128, :],
                                         in_=t1t(k)), 16)
            for k in range(2):
                S.inc(scalar.activation(f_db[:, k * C:(k + 1) * C], t1t(k), AF.Square, accum_out=st(k, 1)))
        sd_t1.append(T1S.n)
        s_su2 = S.n

        @blk.vector
        def _(vector, s_su2=s_su2):
            vector.wait_ge(S.sem, s_su2)
            vstats(vector, range(2))
        v_st3 = V.n

        @blk.scalar
        def _(scalar, v_st3=v_st3):
            scalar.wait_ge(V.sem, v_st3)
            slnexp(scalar, range(2))
        s_st3 = S.n

        @blk.vector
        def _(vector, j0=j0, s_st3=s_st3, t1t=t1t):
            vector.wait_ge(S.sem, s_st3)
            vnorm(vector, range(2), t1t,
                  lambda k: f_t1all[:, (j0 + k) * C:(j0 + k + 1) * C])
        v_t1 = V.n
        v_t1_list.append(v_t1)



    if _STAGE == 35:
        @blk.gpsimd
        def _(gpsimd):
            gpsimd.wait_ge(T1S.sem, sd_t1[-1])
            gpsimd.wait_ge(V.sem, v_t1_list[-1])
            for j in range(16):
                OUTS.inc(gpsimd.dma_start(out=out[j * 128:(j + 1) * 128, :],
                                          in_=x1d[j * 128:(j + 1) * 128, :]), 16)
            gpsimd.wait_ge(OUTS.sem, OUTS.n)
        bctx.close()
        gctx.close()
        return nc

    v_b2ev = [0, 0]
    p_b2tr = [0, 0]
    for ci in range(8):
        for j in range(16):
            @blk.tensor
            def _(tensor, ci=ci, j=j):
                if ci == 0:
                    tensor.wait_ge(V.sem, v_t1_list[j // 2 + 1])
                tensor.wait_ge(V.sem, v_b2ev[-2])
                _mm = tensor.transpose(
                    (psT0 if j % 2 == 0 else psT1)[:, :],
                    f_t1all[:, j * C + ci * 128:j * C + (ci + 1) * 128],
                    c_identbf[:])
                P.inc(_mm)
            p_b2tr.append(P.n)

            @blk.vector
            def _(vector, ci=ci, j=j, pt=P.n):
                vector.wait_ge(P.sem, pt)
                V.inc(vector.tensor_copy(
                    f_t1T[:, j * 128:(j + 1) * 128],
                    (psT0 if j % 2 == 0 else psT1)[:, :]))
            v_b2ev.append(V.n)

        @blk.tensor
        def _(tensor, ci=ci, vev=V.n):
            tensor.wait_ge(V.sem, vev)
            for tq in range(4):
                _mm = tensor.matmul(psA[tq][0:8, :], c_centg[:, ci * E:(ci + 1) * E],
                                    f_t1T[:, tq * TC:(tq + 1) * TC],
                                    start=(ci == 0), stop=(ci == 7))
            P.inc(_mm)
    p_scm_all = P.n

    @blk.vector
    def _(vector, pp=P.n):
        vector.wait_ge(P.sem, pp)
        for tq in range(4):
            V.inc(vector.tensor_scalar(f_sc[:, tq * TC:(tq + 1) * TC], psA[tq][0:8, :],
                                       c_sb8[:], None, ALU.add))
    v_sc = V.n

    if _STAGE == 45:
        @blk.gpsimd
        def _(gpsimd):
            gpsimd.wait_ge(V.sem, v_sc)
            for j in range(16):
                OUTS.inc(gpsimd.dma_start(out=out[j * 128:(j + 1) * 128, :],
                                          in_=x1d[j * 128:(j + 1) * 128, :]), 16)
            gpsimd.wait_ge(OUTS.sem, OUTS.n)
        bctx.close()
        gctx.close()
        return nc

    @blk.gpsimd
    def _(gpsimd):
        gpsimd.wait_ge(V.sem, v_sc)
        G0.inc(gpsimd.dma_start(out=scores_in, in_=f_sc[:]), 16)
        gpsimd.wait_ge(G0.sem, G0.n)
        CC.inc(gpsimd.collective_compute(
            "AllGather", ALU.bypass, replica_groups=GRP8,
            ins=[scores_in.opt()], outs=[scores_g.opt()]))
    cc_sc = CC.n

    if _STAGE <= 4:
        @blk.gpsimd
        def _(gpsimd):
            gpsimd.wait_ge(CC.sem, cc_sc)
            for j in range(16):
                OUTS.inc(gpsimd.dma_start(out=out[j * 128:(j + 1) * 128, :],
                                          in_=x1d[j * 128:(j + 1) * 128, :]), 16)
            gpsimd.wait_ge(OUTS.sem, OUTS.n)
        bctx.close()
        gctx.close()
        return nc

    @blk.vector
    def _(vector):
        V.inc(vector.memset(f_claim[:], 0.0))
        V.inc(vector.memset(f_rall[:], 0.0))
    v_ffninit = V.n

    gd_scatter = []
    cc_routed = []
    p_trm_list = [0]
    regbox = {}

    @blk.vector
    def _(vector):
        vector.wait_ge(V.sem, v_ffninit)
        V.inc(vector.memset(f_db[:], 0.0))
    v_zfill = V.n

    @blk.gpsimd
    def _(gpsimd):
        regbox["bnd"] = gpsimd.alloc_register(name="bndreg")
        gpsimd.reg_mov(regbox["bnd"], E * PAD - 1)
        gpsimd.wait_ge(V.sem, v_zfill)
        nrow = E * PAD + 64
        for r0 in range(0, nrow, 128):
            SCT.inc(gpsimd.dma_start(out=routed_loc[r0:min(r0 + 128, nrow), :],
                                     in_=f_db[0:min(128, nrow - r0), 0:C]), 16)

    sct_zfill = SCT.n
    crw_list = [0]
    v_sel_list = []
    for e in range(E):
        @blk.sync
        def _(sync, e=e):
            sync.wait_ge(CC.sem, cc_sc)
            if e > 0:
                sync.wait_ge(V.sem, v_sel_list[-1])
            ZL.inc(sync.dma_start(
                out=f_Z[:],
                in_=scores_g.rearrange("(c e) (pr f) -> e c pr f", e=8, f=128)[e].opt()), 16)
        ld_ze = ZL.n

        @blk.vector
        def _(vector, e=e, ld_ze=ld_ze):
            vector.wait_ge(ZL.sem, ld_ze)
            if e == 0:
                vector.wait_ge(V.sem, v_ffninit)
            w = V.inc(vector.scalar_tensor_tensor(f_Z[:], f_claim[:], NEG, f_Z[:],
                                                  ALU.mult, ALU.add))
            V.inc(vector.memset(f_loC[:], 0.0))
            vector.wait_ge(V.sem, w + 1)
            V.inc(vector.tensor_scalar(f_dmp[:], f_Z[:], f_loC[:], 0.0,
                                       ALU.is_gt, ALU.add, accum_out=f_rowc[:]))
        v_cmp = V.n
        for it in range(BIS):
            d = 64.0 * (0.5 ** it)

            @blk.tensor
            def _(tensor, v_cmp=v_cmp):
                tensor.wait_ge(V.sem, v_cmp)
                _mm = tensor.matmul(psB[0][:, 0:1], c_ones128[:], f_rowc[:],
                              start=True, stop=True)
                P.inc(_mm)
            p_cn = P.n

            @blk.vector
            def _(vector, it=it, d=d, p_cn=p_cn, last=(it == BIS - 1)):
                vector.wait_ge(P.sem, p_cn)
                w = V.inc(vector.tensor_scalar(f_step[:], psB[0][:, 0:1], 2048.0, d,
                                               ALU.is_ge, ALU.mult))
                vector.wait_ge(V.sem, w)
                w = V.inc(vector.scalar_tensor_tensor(f_loC[:], f_step[:], -d / 2,
                                                      f_loC[:], ALU.add, ALU.add))
                vector.wait_ge(V.sem, w)
                if last:
                    w = V.inc(vector.tensor_scalar(f_self[:], f_Z[:], f_loC[:], 0.0,
                                                   ALU.is_gt, ALU.add,
                                                   accum_out=f_rowc[:]))
                    vector.wait_ge(V.sem, w)
                    V.inc(vector.tensor_max(f_claim[:], f_claim[:], f_self[:]))
                else:
                    V.inc(vector.tensor_scalar(f_dmp[:], f_Z[:], f_loC[:], 0.0,
                                               ALU.is_gt, ALU.add,
                                               accum_out=f_rowc[:]))
            v_cmp = V.n
        v_sel = V.n
        v_sel_list.append(v_sel)

        @blk.tensor
        def _(tensor, v_sel=v_sel):
            tensor.wait_ge(V.sem, v_sel)
            _mm = tensor.matmul(psB[0][0:8, 2:3], c_blockind[:], f_rowc[:], start=True, stop=True)
            _mm = tensor.matmul(psB[0][:, 3:4], c_ltri[:], f_rowc[:], start=True, stop=True)
            P.inc(_mm)
        p_cnt = P.n

        @blk.vector
        def _(vector, e=e, p_cnt=p_cnt):
            vector.wait_ge(P.sem, p_cnt)
            V.inc(vector.tensor_copy(f_cnt8[:, e:e + 1], psB[0][0:8, 2:3]))
        v_cnt8 = V.n

        @blk.tensor
        def _(tensor, e=e, v_cnt8=v_cnt8):
            tensor.wait_ge(V.sem, v_cnt8)
            _mm = tensor.matmul(psB[0][0:1, 4:12], f_cnt8[:, e:e + 1], c_ident16[0:8, 0:8],
                          start=True, stop=True)
            _mm = tensor.matmul(psB[0][0:1, 12:13], f_cnt8[:, e:e + 1], c_cmask[:],
                          start=True, stop=True)
            P.inc(_mm)
        p_tr8 = P.n

        @blk.vector
        def _(vector, e=e, p_tr8=p_tr8):
            vector.wait_ge(P.sem, p_tr8)
            vector.wait_ge(CRW.sem, crw_list[-1])
            w = V.inc(vector.tensor_copy(f_crow[0:1, 0:8], psB[0][0:1, 4:12]))
            vector.wait_ge(V.sem, w)
            for sh in (1, 2, 4):
                w = V.inc(vector.tensor_copy(f_crow[0:1, 16:24], f_crow[0:1, 0:8]))
                vector.wait_ge(V.sem, w)
                w = V.inc(vector.tensor_add(f_crow[0:1, sh:8], f_crow[0:1, sh:8],
                                            f_crow[0:1, 16:24 - sh]))
                vector.wait_ge(V.sem, w)
            w = V.inc(vector.tensor_scalar(f_crow[0:1, 8:16], psB[0][0:1, 4:12], -1.0,
                                           float(PAD), ALU.mult, ALU.add))
            vector.wait_ge(V.sem, w)
            V.inc(vector.tensor_copy(f_cc[:], psB[0][0:1, 12:13]))
        v_crow = V.n

        @blk.gpsimd
        def _(gpsimd, e=e, v_crow=v_crow):
            gpsimd.wait_ge(V.sem, v_crow)
            CRW.inc(gpsimd.dma_start(out=crow_bounce[e * 16:(e + 1) * 16, 0:1],
                                    in_=f_crow[0:1, 0:16]), 16)
        crw_list.append(CRW.n)

        @blk.vector
        def _(vector, v_sel=v_sel, p_cnt=p_cnt):
            vector.wait_ge(V.sem, v_sel)
            vector.wait_ge(P.sem, p_cnt)
            w = V.inc(vector.tensor_tensor_scan(
                f_rank[:], c_ones[:, 0:1].to_broadcast([128, 128]), f_self[:],
                0.0, ALU.mult, ALU.add))
            vector.wait_ge(V.sem, w)
            w = V.inc(vector.tensor_sub(f_rank[:], f_rank[:], f_self[:]))
            vector.wait_ge(V.sem, w)
            V.inc(vector.tensor_scalar_add(f_rank2[:], f_rank[:], psB[0][:, 3:4]))
        v_rank = V.n

        @blk.tensor
        def _(tensor, v_crow=v_crow):
            tensor.wait_ge(V.sem, v_crow)
            _mm = tensor.matmul(psB[0][:, 13:14], c_ones1r[:], f_cc[:], start=True, stop=True)
            P.inc(_mm)
        p_cm = P.n

        @blk.vector
        def _(vector, e=e, v_rank=v_rank, p_cm=p_cm):
            vector.wait_ge(V.sem, v_rank)
            vector.wait_ge(P.sem, p_cm)
            vector.wait_ge(P.sem, p_trm_list[-1])
            w = V.inc(vector.tensor_scalar(f_slotm[:, 0:128], f_rank2[:], 1.0,
                                           float(PAD * e) + 1e6, ALU.mult, ALU.add))
            vector.wait_ge(V.sem, w)
            w = V.inc(vector.scalar_tensor_tensor(f_slotm[:, 0:128], f_self[:], -1e6,
                                                  f_slotm[:, 0:128], ALU.mult, ALU.add))
            vector.wait_ge(V.sem, w)
            w = V.inc(vector.tensor_scalar_min(f_slotm[:, 0:128], f_slotm[:, 0:128],
                                               float(E * PAD + 8)))
            w2 = V.inc(vector.tensor_scalar_add(f_slotm[:, 128:256], f_rank2[:],
                                                psB[0][:, 13:14]))
            vector.wait_ge(V.sem, w2)
            for th in (512.0, 4608.0, 8704.0):
                w2 = V.inc(vector.tensor_scalar(f_rank[:], f_slotm[:, 128:256],
                                                th, 3584.0, ALU.is_ge, ALU.mult))
                vector.wait_ge(V.sem, w2)
                w2 = V.inc(vector.tensor_add(f_slotm[:, 128:256],
                                             f_slotm[:, 128:256], f_rank[:]))
                vector.wait_ge(V.sem, w2)
            V.inc(vector.scalar_tensor_tensor(f_slotm[:, 128:256],
                                              f_slotm[:, 128:256], 512.0 * e,
                                              f_self[:], ALU.add, ALU.mult))
        v_slot = V.n

        @blk.tensor
        def _(tensor, v_slot=v_slot):
            tensor.wait_ge(V.sem, v_slot)
            _mm = tensor.matmul(psB[0][:, 16:32], f_slotm[:, 0:128], c_selmat[:],
                          start=True, stop=True)
            _mm = tensor.matmul(psB[0][:, 32:48], f_slotm[:, 128:256], c_selmat[:],
                          start=True, stop=True)
            P.inc(_mm)
        p_trm = P.n
        p_trm_list.append(p_trm)

        @blk.vector
        def _(vector, e=e, p_trm=p_trm):
            vector.wait_ge(P.sem, p_trm)
            if gd_scatter:
                vector.wait_ge(SCT.sem, gd_scatter[-1])
            V.inc(vector.tensor_copy(f_sloti[:], psB[0][:, 16:32]))
            V.inc(vector.tensor_add(f_rall[:], f_rall[:], psB[0][:, 32:48]))
        v_sli = V.n

        @blk.gpsimd
        def _(gpsimd, e=e, v_sli=v_sli):
            gpsimd.wait_ge(V.sem, v_sli)
            gpsimd.wait_ge(V.sem, v_t1_list[-1])
            gpsimd.wait_ge(SCT.sem, sct_zfill)
            for j in range(16):
                SCT.inc(gpsimd.indirect_dma_start(
                    routed_loc, IndirectOffsetOnAxis(f_sloti[:, j:j + 1], 0),
                    f_t1all[:, j * C:(j + 1) * C], None,
                    bounds_check=regbox["bnd"], oob_is_err=False), 16)
        gd_scatter.append(SCT.n)

        @blk.gpsimd
        def _(gpsimd, e=e):
            gpsimd.wait_ge(SCT.sem, gd_scatter[-1])
            CC.inc(gpsimd.collective_compute(
                "AllGather", ALU.bypass, replica_groups=GRP8,
                ins=[routed_loc[e * PAD:(e + 1) * PAD, :].opt()],
                outs=[routed_gall[e * 8 * PAD:(e + 1) * 8 * PAD, :].opt()]))
        cc_routed.append(CC.n)

    @blk.vector
    def _(vector, vv=V.n):
        vector.wait_ge(V.sem, vv)
        w = V.inc(vector.tensor_scalar_min(f_idx2t[:], f_rall[:], float(8 * TH - 1)))
        vector.wait_ge(V.sem, w)
        V.inc(vector.tensor_copy(f_ralli[:], f_idx2t[:]))
    v_ralli = V.n

    @blk.gpsimd
    def _(gpsimd, vv=V.n):
        gpsimd.wait_ge(V.sem, vv)
        gpsimd.wait_ge(CRW.sem, CRW.n)
        CRW.inc(gpsimd.dma_start(out=f_crowcol[:], in_=crow_bounce), 16)
    gd_cm = CRW.n

    @blk.tensor
    def _(tensor, gd_cm=gd_cm):
        tensor.wait_ge(CRW.sem, gd_cm)
        _mm = tensor.matmul(psB[0][0:1, 48:64], f_crowcol[:], c_selmat[:],
                      start=True, stop=True)
        P.inc(_mm)
    p_cmA = P.n

    @blk.vector
    def _(vector, p_cmA=p_cmA):
        vector.wait_ge(P.sem, p_cmA)
        V.inc(vector.tensor_copy(f_cmine[:], psB[0][0:1, 48:64]))
    v_cmine = V.n

    @blk.tensor
    def _(tensor, v_cmine=v_cmine):
        tensor.wait_ge(V.sem, v_cmine)
        _mm = tensor.matmul(psB[0][:, 48:64], c_ones1r[:], f_cmine[:], start=True, stop=True)
        _mm = tensor.matmul(psB[0][:, 64:65], c_ones1r[:], c_expoff[:], start=True, stop=True)
        P.inc(_mm)
    p_cmb = P.n

    @blk.vector
    def _(vector, p_cmb=p_cmb):
        vector.wait_ge(P.sem, p_cmb)
        w = V.inc(vector.tensor_copy(f_cumrep[:, 0:16], psB[0][:, 48:64]))
        V.inc(vector.tensor_copy(f_cumrep[:, 16:17], psB[0][:, 64:65]))
        vector.wait_ge(V.sem, w + 1)
        w = V.inc(vector.tensor_scalar_add(f_idx2f[:], c_stile[:], f_cumrep[:, 16:17]))
        vector.wait_ge(V.sem, w)
        for cb in range(7):
            w = V.inc(vector.tensor_scalar(f_idx2t[:], c_stile[:],
                                           f_cumrep[:, cb:cb + 1],
                                           f_cumrep[:, 8 + cb:9 + cb],
                                           ALU.is_ge, ALU.mult))
            vector.wait_ge(V.sem, w)
            w = V.inc(vector.tensor_add(f_idx2f[:], f_idx2f[:], f_idx2t[:]))
            vector.wait_ge(V.sem, w)
        w = V.inc(vector.tensor_scalar_min(f_idx2t[:], f_idx2f[:],
                                           float(E * 8 * PAD - 1)))
        vector.wait_ge(V.sem, w)
        V.inc(vector.tensor_copy(f_idx2i[:], f_idx2t[:]))
    v_idx2 = V.n

    p_tr_list = [0, 0]
    v_ev_list = [0, 0]
    gd_gr_list = [0]
    for j in range(16):
        @blk.gpsimd
        def _(gpsimd, j=j):
            if j == 0:
                gpsimd.wait_ge(CC.sem, cc_routed[-1])
                gpsimd.wait_ge(V.sem, v_idx2)
            else:
                gpsimd.wait_ge(P.sem, p_tr_list[-1])
            GTH.inc(gpsimd.indirect_dma_start(
                f_grows[:], None,
                routed_gall, IndirectOffsetOnAxis(f_idx2i[:, j:j + 1], 0)), 16)
        gd_gr_list.append(GTH.n)

        for ci in range(8):
            @blk.tensor
            def _(tensor, j=j, ci=ci, gg=gd_gr_list[-1]):
                tensor.wait_ge(GTH.sem, gg)
                tensor.wait_ge(V.sem, v_ev_list[-2])
                _mm = tensor.transpose((psT0 if ci % 2 == 0 else psT1)[:, :],
                                 f_grows[:, ci * 128:(ci + 1) * 128],
                                 c_identbf[:])
                P.inc(_mm)
            p_tr_list.append(P.n)

            @blk.vector
            def _(vector, j=j, ci=ci, pt=P.n):
                vector.wait_ge(P.sem, pt)
                V.inc(vector.tensor_scalar(
                    f_normT[:, ci * TH + j * 128:ci * TH + (j + 1) * 128],
                    (psT0 if ci % 2 == 0 else psT1)[:, :],
                    c_lngb[:, ci:ci + 1], c_lngb[:, 8 + ci:9 + ci],
                    ALU.mult, ALU.add))
            v_ev_list.append(V.n)
    v_normT = V.n

    sd_h = []
    p_h2 = [0]
    p_h1g = [0, 0, 0, 0]
    s_gelu_all = [0, 0]
    p_h2_tq_end = [0]
    v_h2ev = [0, 0]
    v_b1_list = [0, 0]
    def emit_hgather(t):
        @blk.gpsimd
        def _(gpsimd, t=t):
            gpsimd.wait_ge(HS.sem, hs_tq_end[t])
            CC.inc(gpsimd.collective_compute(
                "AllGather", ALU.bypass, replica_groups=GRP8,
                ins=[h_loc[t * 512:(t + 1) * 512, :].opt()],
                outs=[h_g[t * 4096:(t + 1) * 4096, :].opt()]))

    hs_tq_end = []
    p_w2h0 = [0]
    s_gelu_half = {}
    w1sems = [W1L, W1M, W1N, W1P]
    for tq in range(4):
        if tq >= 2:
            emit_hgather(tq - 2)
        for ft in range(32):
            wbuf = (ft % 4) * (8 * 128)

            wsem = w1sems[ft % 4]

            @blk.gpsimd
            def _(gpsimd, tq=tq, ft=ft, wbuf=wbuf, wsem=wsem):
                gpsimd.wait_ge(P.sem, p_h1g[-4])
                if tq == 0 and ft == 0:
                    gpsimd.wait_ge(V.sem, v_normT)
                wsem.inc(gpsimd.dma_start(
                    out=f_w1t[:, wbuf:wbuf + 8 * 128].rearrange("p (a f) -> p a f", a=8),
                    in_=w1.rearrange("(a p) f -> p a f", p=128)[:, :, ft * 128:(ft + 1) * 128]), 16)
            gd_w1 = wsem.n

            @blk.tensor
            def _(tensor, tq=tq, ft=ft, wbuf=wbuf, gd_w1=gd_w1):
                tensor.wait_ge(wsem.sem, gd_w1)
                if tq == 0 and ft == 0:
                    tensor.wait_ge(V.sem, v_normT)
                tensor.wait_ge(S.sem, s_gelu_all[-2])
                bank = psA[ft % 2]
                for ci in range(8):
                    _mm = tensor.matmul(bank[:, :],
                                  f_w1t[:, wbuf + ci * 128:wbuf + (ci + 1) * 128],
                                  f_normT[:, ci * TH + tq * TC:ci * TH + (tq + 1) * TC],
                                  start=(ci == 0), stop=(ci == 7))
                if tq == 0:
                    bc = 65 + (ft % 2)
                    if ft >= 2:
                        tensor.wait_ge(V.sem, v_b1_list[-2])
                    for ci in range(8):
                        _mm = tensor.matmul(psB[1][:, bc:bc + 1],
                                      f_w1t[:, wbuf + ci * 128:wbuf + (ci + 1) * 128],
                                      c_lnbb[:, ci:ci + 1],
                                      start=(ci == 0), stop=(ci == 7))
                P.inc(_mm)
            p_h1 = P.n
            p_h1g.append(p_h1)

            if tq == 0:
                @blk.vector
                def _(vector, ft=ft, p_h1=p_h1):
                    vector.wait_ge(P.sem, p_h1)
                    V.inc(vector.tensor_add(f_b1e[:, ft:ft + 1],
                                            psB[1][:, 65 + (ft % 2):66 + (ft % 2)],
                                            c_b1c[:, ft:ft + 1]))
                v_b1_list.append(V.n)
            v_b1 = V.n

            @blk.scalar
            def _(scalar, tq=tq, ft=ft, p_h1=p_h1, v_b1=v_b1):
                scalar.wait_ge(P.sem, p_h1)
                scalar.wait_ge(V.sem, v_b1)
                scalar.wait_ge(P.sem, p_h2_tq_end[-1])
                S.inc(scalar.activation(f_h[:, ft * TC:(ft + 1) * TC],
                                        psA[ft % 2][:, :], AF.Gelu_apprx_tanh,
                                        bias=f_b1e[:, ft:ft + 1]))
            s_gelu_all.append(S.n)
            if ft == 15:
                s_gelu_half[tq] = S.n

        s_gelu_end = S.n
        for do in range(2):
            @blk.gpsimd
            def _(gpsimd, do=do):
                gpsimd.wait_ge(P.sem, p_w2h0[-1])
                W2L.inc(gpsimd.dma_start(
                    out=f_w2r[:, 0:16 * TC].rearrange("p (a f) -> p a f", a=16),
                    in_=w2.rearrange("(a p) f -> p a f", p=128)[:, 0:16, do * TC:(do + 1) * TC]), 16)
            gd_w2h0 = W2L.n

            @blk.gpsimd
            def _(gpsimd, do=do, pp=p_h2[-1]):
                gpsimd.wait_ge(P.sem, pp)
                W2H.inc(gpsimd.dma_start(
                    out=f_w2r[:, 16 * TC:32 * TC].rearrange("p (a f) -> p a f", a=16),
                    in_=w2.rearrange("(a p) f -> p a f", p=128)[:, 16:32, do * TC:(do + 1) * TC]), 16)
            gd_w2h1 = W2H.n

            for tsub in range(4):
                @blk.tensor
                def _(tensor, tsub=tsub, gd_w2h0=gd_w2h0, gd_w2h1=gd_w2h1,
                      sg0=s_gelu_half[tq], sg1=s_gelu_end):
                    tensor.wait_ge(W2L.sem, gd_w2h0)
                    tensor.wait_ge(S.sem, sg0)
                    tensor.wait_ge(V.sem, v_h2ev[-2])
                    bank = psB[tsub % 2]
                    for ft in range(16):
                        _mm = tensor.matmul(bank[:, :],
                                      f_h[:, ft * TC + tsub * 128:ft * TC + (tsub + 1) * 128],
                                      f_w2r[:, ft * TC:(ft + 1) * TC],
                                      start=(ft == 0), stop=False)
                    if tsub == 3:
                        P.inc(_mm)
                        p_w2h0.append(P.n)
                    tensor.wait_ge(W2H.sem, gd_w2h1)
                    tensor.wait_ge(S.sem, sg1)
                    for ft in range(16, 32):
                        _mm = tensor.matmul(bank[:, :],
                                      f_h[:, ft * TC + tsub * 128:ft * TC + (tsub + 1) * 128],
                                      f_w2r[:, ft * TC:(ft + 1) * TC],
                                      start=False, stop=(ft == 31))
                    P.inc(_mm)
                p_h2.append(P.n)

                @blk.vector
                def _(vector, do=do, tsub=tsub, pp=P.n):
                    vector.wait_ge(P.sem, pp)
                    if sd_h:
                        vector.wait_ge(HS.sem, sd_h[-1])
                    V.inc(vector.tensor_add(f_hrow[:, do * TC:(do + 1) * TC],
                                            psB[tsub % 2][:, :],
                                            c_b2rep[:, do * TC:(do + 1) * TC]))
                v_h2ev.append(V.n)

                @blk.scalar
                def _(scalar, tq=tq, do=do, tsub=tsub, vv=V.n):
                    scalar.wait_ge(V.sem, vv)
                    row = tq * TC + tsub * 128
                    HS.inc(scalar.dma_start(
                        out=h_loc[row:row + 128, do * TC:(do + 1) * TC],
                        in_=f_hrow[:, do * TC:(do + 1) * TC]), 16)
                sd_h.append(HS.n)
        p_h2_tq_end.append(P.n)
        hs_tq_end.append(HS.n)

    emit_hgather(2)
    emit_hgather(3)
    cc_h = CC.n

    def ex1(k):
        return f_x1g[:, k * C:(k + 1) * C]

    def et1(k):
        return f_x1g[:, (2 + k) * C:(3 + k) * C]

    def ehm(k):
        return f_hmg[:, k * C:(k + 1) * C]

    def effo(k):
        return f_t1bg[:, k * C:(k + 1) * C]

    ld_fin = []
    v_fgrp = []
    v_f1g = []
    for g in range(8):
        j0 = 2 * g

        @blk.sync
        def _(sync, g=g, j0=j0):
            if g == 0:
                sync.wait_ge(T1S.sem, sd_t1[-1])
                sync.wait_ge(V.sem, v_t1_list[-1])
            else:
                sync.wait_ge(V.sem, v_fgrp[-1])
            for k in range(2):
                FXL.inc(sync.dma_start(out=ex1(k),
                                       in_=x1d[(j0 + k) * 128:(j0 + k + 1) * 128, :]), 16)
                T1L.inc(sync.dma_start(out=et1(k),
                                       in_=t1d[(j0 + k) * 128:(j0 + k + 1) * 128, :]), 16)
        ld_x1f = FXL.n
        ld_t1f = T1L.n

        @blk.gpsimd
        def _(gpsimd, g=g, j0=j0):
            gpsimd.wait_ge(CC.sem, cc_h)
            gpsimd.wait_ge(V.sem, v_ralli)
            if g > 0:
                gpsimd.wait_ge(V.sem, v_f1g[-1])
            for k in range(2):
                HM.inc(gpsimd.indirect_dma_start(
                    ehm(k), None,
                    h_g, IndirectOffsetOnAxis(f_ralli[:, j0 + k:j0 + k + 1], 0)), 16)
        gd_hm = HM.n

        @blk.vector
        def _(vector, g=g, ld_x1f=ld_x1f, ld_t1f=ld_t1f, gd_hm=gd_hm):
            vector.wait_ge(FXL.sem, ld_x1f)
            vector.wait_ge(T1L.sem, ld_t1f)
            vector.wait_ge(HM.sem, gd_hm)
            vector.wait_ge(OUTS.sem, ld_fin[-1] if ld_fin else 0)
            for k in range(2):
                V.inc(vector.tensor_add(effo(k), et1(k), ehm(k)))
            vector.wait_ge(V.sem, V.n)
            for k in range(2):
                V.inc(vector.tensor_scalar(f_db[:, k * C:(k + 1) * C], effo(k), 0.0, 0.0, ALU.add, ALU.add,
                                           accum_out=st(k, 0)))
        v_f1 = V.n
        v_f1g.append(v_f1)

        @blk.scalar
        def _(scalar, v_f1=v_f1):
            scalar.wait_ge(V.sem, v_f1)
            for k in range(2):
                S.inc(scalar.activation(f_db[:, k * C:(k + 1) * C], effo(k), AF.Square, accum_out=st(k, 1)))
        s_f1 = S.n

        @blk.vector
        def _(vector, s_f1=s_f1):
            vector.wait_ge(S.sem, s_f1)
            vstats(vector, range(2))
        v_f2 = V.n

        @blk.scalar
        def _(scalar, v_f2=v_f2):
            scalar.wait_ge(V.sem, v_f2)
            slnexp(scalar, range(2))
        s_f2 = S.n

        @blk.vector
        def _(vector, g=g, s_f2=s_f2):
            vector.wait_ge(S.sem, s_f2)
            vnorm(vector, range(2), effo, effo)
            vector.wait_ge(V.sem, V.n)
            for k in range(2):
                V.inc(vector.tensor_add(effo(k), effo(k), ex1(k)))
        v_f3 = V.n
        v_fgrp.append(v_f3)

        @blk.sync
        def _(sync, j0=j0, v_f3=v_f3):
            sync.wait_ge(V.sem, v_f3)
            for k in range(2):
                OUTS.inc(sync.dma_start(out=out[(j0 + k) * 128:(j0 + k + 1) * 128, :],
                                        in_=effo(k)), 16)
        ld_fin.append(OUTS.n)

    @blk.gpsimd
    def _(gpsimd):
        gpsimd.wait_ge(OUTS.sem, ld_fin[-1])

    bctx.close()
    gctx.close()
    return nc


_NC_CACHE = {}


def _get_nc():
    key = _STAGE
    if key not in _NC_CACHE:
        _NC_CACHE[key] = build_nc()
    return _NC_CACHE[key]


def kernel(**inputs):
    x = np.asarray(inputs["L0_attn_x"], np.float32)
    td = np.asarray(inputs["L0_attn_time_decay"], np.float32).reshape(-1)
    tf = np.asarray(inputs["L0_attn_time_first"], np.float32).reshape(-1)
    tmk = np.asarray(inputs["L0_attn_time_mix_k"], np.float32).reshape(-1)
    tmv = np.asarray(inputs["L0_attn_time_mix_v"], np.float32).reshape(-1)
    tmr = np.asarray(inputs["L0_attn_time_mix_r"], np.float32).reshape(-1)
    Wk = np.asarray(inputs["L0_attn_Wk"], np.float32)
    Wv = np.asarray(inputs["L0_attn_Wv"], np.float32)
    Wr = np.asarray(inputs["L0_attn_Wr"], np.float32)
    Wo = np.asarray(inputs["L0_attn_Wo"], np.float32)
    cent = np.asarray(inputs["L0_ffn_centroids"], np.float32)
    lng = np.asarray(inputs["L0_ffn_ln_g"], np.float32)
    lnb = np.asarray(inputs["L0_ffn_ln_b"], np.float32)
    w1 = np.asarray(inputs["L0_ffn_w1"], np.float32)
    b1 = np.asarray(inputs["L0_ffn_b1"], np.float32)
    w2 = np.asarray(inputs["L0_ffn_w2"], np.float32)
    b2 = np.asarray(inputs["L0_ffn_b2"], np.float32)

    WkT = np.ascontiguousarray(Wk.T)
    WvT = np.ascontiguousarray(Wv.T)
    WrT = np.ascontiguousarray(Wr.T)
    WoT = np.ascontiguousarray(Wo.T)
    centT = np.ascontiguousarray(cent.T)

    def tile_col(vec):
        return np.ascontiguousarray(vec.reshape(8, 128).T)

    blockind = np.zeros((128, 8), np.float32)
    for p in range(128):
        blockind[p, p // 16] = 1.0
    ltri = np.zeros((128, 128), np.float32)
    for m in range(128):
        for p in range((m // 16) * 16, m):
            ltri[p, m] = 1.0
    stl = (np.arange(16)[None, :] * 128 + np.arange(128)[:, None]).astype(np.float32)
    ident16 = np.eye(16, dtype=np.float32)
    identbf = np.eye(128, dtype=np.float32).astype(ml_dtypes.bfloat16)

    in_maps = []
    for c in range(8):
        b, h = c // 2, c % 2
        tdh = td[h * DH:(h + 1) * DH]
        tfh = tf[h * DH:(h + 1) * DH]
        tdtf = np.concatenate([
            np.ascontiguousarray(tdh.reshape(4, 128).T),
            np.ascontiguousarray(tfh.reshape(4, 128).T)], axis=1)
        tmix = np.concatenate([tile_col(tmk), tile_col(tmv), tile_col(tmr)], axis=1)
        lngb_ = np.concatenate([tile_col(lng), tile_col(lnb)], axis=1)
        b1c = np.ascontiguousarray(b1[c].reshape(32, 128).T)
        m = {
            "xT": np.ascontiguousarray(x[b].T),
            "xres": np.ascontiguousarray(x[b, h * TH:(h + 1) * TH]),
            "wkT": np.ascontiguousarray(WkT[:, h * DH:(h + 1) * DH]),
            "wvT": np.ascontiguousarray(WvT[:, h * DH:(h + 1) * DH]),
            "wrT": np.ascontiguousarray(WrT[:, h * DH:(h + 1) * DH]),
            "woT": WoT,
            "tdtf": np.ascontiguousarray(tdtf),
            "tmix": np.ascontiguousarray(tmix),
            "lngb": np.ascontiguousarray(lngb_),
            "lnbb": np.ascontiguousarray(tile_col(lnb).astype(ml_dtypes.bfloat16)),
            "centT": centT,
            "w1": np.ascontiguousarray(w1[c]),
            "b1cc": b1c,
            "w2": np.ascontiguousarray(w2[c]),
            "b2row": np.ascontiguousarray(b2[c].reshape(1, C)),
            "blockind": blockind,
            "ltri16": ltri,
            "cmask": (np.arange(8) < c).astype(np.float32).reshape(8, 1),
            "stile": stl,
            "onescol": np.ones((128, 1), np.float32),
            "onescolb": np.ones((128, 1), np.float32).astype(ml_dtypes.bfloat16),
            "onesrow1": np.ones((1, 128), np.float32),
            "ones128": np.ones((128, 128), np.float32),
            "identbf": identbf,
            "ident16": ident16,
            "epscol": np.full((128, 1), EPS, np.float32),
            "wkvidx": (h * 4096 + (np.arange(32)[None, :] * 128)
                       + np.arange(128)[:, None]).astype(np.int32),
            "rowidx": (c * 16 + np.arange(16)[:, None]).astype(np.int32),
                        "selmat": np.eye(128, dtype=np.float32)[:, c * 16:c * 16 + 16].copy(),
            "expoff": np.array([[c * 8 * PAD]], np.float32),
        }
        in_maps.append(m)

    nc = _get_nc()
    res = run_bass_kernel_spmd(nc, in_maps, core_ids=list(range(8)))
    outp = np.zeros((B, T, C), np.float32)
    for c in range(8):
        b, h = c // 2, c % 2
        outp[b, h * TH:(h + 1) * TH] = res.results[c]["out"]
    return outp
